# version 02
# baseline (speedup 1.0000x reference)
import contextlib
import math
import os

ATTN_DBG = int(os.environ.get("ATTN_DBG", "4"))
ATTN_SUB = os.environ.get("ATTN_SUB", "kvqor")
ATTN_ROPE = int(os.environ.get("ATTN_ROPE", "5"))
ATTN_QH = int(os.environ.get("ATTN_QH", "4"))

import numpy as np

import concourse.bass as bass
import concourse.mybir as mybir
from concourse.bass_utils import run_bass_kernel_spmd

F32 = mybir.dt.float32
BF16 = mybir.dt.bfloat16
AF = mybir.ActivationFunctionType
ALU = mybir.AluOpType
AX = mybir.AxisListType

D = 2048
DCH = 16
FF = 1408
FCH = 11
NTOK = 2048
FT = 1024
AT = 512
NR = 16
NS = 2
KR = 4
H = 16
HKV = 4
DH = 128
CH = 64
NCORE = 8

POOL_LANES = tuple(f"pool{i}" for i in range(8))
SYNC_LANES = tuple(f"sync{i}" for i in range(4))
ENGINES = ("pe", "act", "dve") + POOL_LANES + SYNC_LANES + ("cc",)
PHYS = {}
for _e in ("pe", "act", "dve"):
    PHYS[_e] = _e
for _e in POOL_LANES:
    PHYS[_e] = "pool"
for _e in SYNC_LANES:
    PHYS[_e] = "sync"
PHYS["cc"] = "pool"


class Buf:
    __slots__ = ("writer", "readers", "init", "psum")

    def __init__(self, init=(), psum=False):
        self.writer = None
        self.readers = []
        self.init = list(init)
        self.psum = psum


def rebind(P, bufs):
    init = []
    for b in bufs:
        if b.writer is not None:
            init.append(b.writer)
        init.extend(b.readers)
        init.extend(b.init)
    return Buf(init=P.reduce_gids(init))


class Op:
    __slots__ = ("gid", "eng", "seq", "fn", "inc", "deps", "after", "marked",
                 "wait_list", "semval")

    def __init__(self, gid, eng, seq, fn, inc, deps, after):
        self.gid = gid
        self.eng = eng
        self.seq = seq
        self.fn = fn
        self.inc = inc
        self.deps = deps
        self.after = after
        self.marked = False
        self.wait_list = []
        self.semval = 0


class Prog:
    def __init__(self):
        self.ops = []
        self.eng_ops = {e: [] for e in ENGINES}
        self.clock = {e: {e2: -1 for e2 in ENGINES} for e in ENGINES}
        self.rr = {"pool": 0, "sync": 0}
        self.last_on_lane = {}

    def reduce_gids(self, gids):
        best = {}
        for g in gids:
            o = self.ops[g]
            if o.eng not in best or self.ops[best[o.eng]].seq < o.seq:
                best[o.eng] = g
        return list(best.values())

    def op(self, eng, fn, reads=(), writes=(), inc=1):
        extra = ()
        if eng in ("pool", "sync"):
            if inc == 16:
                lanes = POOL_LANES if eng == "pool" else SYNC_LANES
                lane = lanes[self.rr[eng] % len(lanes)]
                self.rr[eng] += 1
                if lane in self.last_on_lane:
                    extra = (self.last_on_lane[lane],)
                self.last_on_lane[lane] = len(self.ops)
                eng = lane
            else:
                eng = (POOL_LANES if eng == "pool" else SYNC_LANES)[0]
        gid = len(self.ops)
        seq = len(self.eng_ops[eng])
        dep_gids = set(extra)
        for b in reads:
            if b.writer is not None:
                dep_gids.add(b.writer)
            dep_gids.update(b.init)
            if b.psum and eng in ("act", "dve"):
                for g in b.readers:
                    if self.ops[g].eng in ("act", "dve") and self.ops[g].eng != eng:
                        dep_gids.add(g)
        for b in writes:
            if b.writer is not None:
                dep_gids.add(b.writer)
            dep_gids.update(b.readers)
            dep_gids.update(b.init)
        cl = self.clock[eng]
        deps = []
        for g in sorted(self.reduce_gids(dep_gids)):
            o = self.ops[g]
            if o.eng == "pe" and eng == "pe":
                continue
            if cl[o.eng] >= o.seq:
                continue
            deps.append(g)
            for e2 in ENGINES:
                if o.after[e2] > cl[e2]:
                    cl[e2] = o.after[e2]
        after = dict(cl)
        after[eng] = seq
        op = Op(gid, eng, seq, fn, inc, deps, after)
        self.ops.append(op)
        self.eng_ops[eng].append(op)
        for b in reads:
            b.readers.append(gid)
        for b in writes:
            b.writer = gid
            b.readers = []
        return gid

    def finalize(self):
        for op in self.ops:
            if op.inc == 16:
                op.marked = True
            for g in op.deps:
                self.ops[g].marked = True
        cum = {e: 0 for e in ENGINES}
        for op in self.ops:
            if op.marked:
                cum[op.eng] += op.inc
                op.semval = cum[op.eng]
        for op in self.ops:
            op.wait_list = [(self.ops[g].eng, self.ops[g].semval)
                            for g in op.deps]
        self.n_marked = sum(1 for o in self.ops if o.marked)

    def replay(self, nc, sems):
        self.finalize()
        with nc.Block() as block:
            def mk(phys):
                ops = sorted((o for o in self.ops if PHYS[o.eng] == phys),
                             key=lambda o: o.gid)

                def body(eng):
                    for op in ops:
                        for (pe_eng, val) in op.wait_list:
                            eng.wait_ge(sems[pe_eng], val)
                        ins = op.fn(eng)
                        if op.marked and ins is not None:
                            ins.then_inc(sems[op.eng], op.inc)
                return body
            block.tensor(mk("pe"))
            block.scalar(mk("act"))
            block.vector(mk("dve"))
            block.gpsimd(mk("pool"))
            block.sync(mk("sync"))



def build_program(mode="full"):
    nc = bass.Bass("TRN2", target_bir_lowering=False, debug=False)

    xin = nc.dram_tensor("xT", [D, NTOK], F32, kind="ExternalInput").ap()
    yout = nc.dram_tensor("y", [D, NTOK], F32, kind="ExternalOutput").ap()
    x1 = nc.dram_tensor("x1d", [D, NTOK], F32).ap()
    x2 = nc.dram_tensor("x2d", [D, NTOK], F32).ap()
    cc_in = nc.dram_tensor("cc_in", [128, H * DH], F32).ap()
    cc_out = nc.dram_tensor("cc_out", [128, H * DH], F32).ap()

    rw1 = nc.dram_tensor("rw1", [NR, FCH, DCH, 128, 128], F32, kind="ExternalInput").ap()
    sw1 = nc.dram_tensor("sw1", [NS, FCH, DCH, 128, 128], F32, kind="ExternalInput").ap()
    rw2 = nc.dram_tensor("rw2", [NR, DCH, FCH, 128, 128], F32, kind="ExternalInput").ap()
    sw2 = nc.dram_tensor("sw2", [NS, DCH, FCH, 128, 128], F32, kind="ExternalInput").ap()
    b1in = nc.dram_tensor("b1a", [128, (NR + NS) * FCH], F32, kind="ExternalInput").ap()
    rb2 = nc.dram_tensor("rb2", [NR, D], F32, kind="ExternalInput").ap()
    sb2 = nc.dram_tensor("sb2", [NS, D], F32, kind="ExternalInput").ap()
    centin = nc.dram_tensor("centT", [DCH, 128, NR], F32, kind="ExternalInput").ap()
    biasin = nc.dram_tensor("rbias", [NR, 1], F32, kind="ExternalInput").ap()
    wq_in = nc.dram_tensor("wqT", [H, DCH, 128, 128], F32, kind="ExternalInput").ap()
    wk_in = nc.dram_tensor("wkT", [HKV, DCH, 128, 128], F32, kind="ExternalInput").ap()
    wv_in = nc.dram_tensor("wvT", [HKV, DCH, 128, 128], F32, kind="ExternalInput").ap()
    wo_in = nc.dram_tensor("woT", [DCH, H, 128, 128], F32, kind="ExternalInput").ap()
    cosin = nc.dram_tensor("costab", [128, NTOK], F32, kind="ExternalInput").ap()
    sinin = nc.dram_tensor("sintab", [128, NTOK], F32, kind="ExternalInput").ap()
    diagin = nc.dram_tensor("diagT", [CH, H * CH], F32, kind="ExternalInput").ap()
    qdecin = nc.dram_tensor("qdec", [128, H * CH], F32, kind="ExternalInput").ap()
    kdecin = nc.dram_tensor("kdec", [CH, H], F32, kind="ExternalInput").ap()
    eyein = nc.dram_tensor("eye", [128, 128], F32, kind="ExternalInput").ap()
    rotin = nc.dram_tensor("rotR", [128, 128], F32, kind="ExternalInput").ap()
    eselin = nc.dram_tensor("esel", [NR, NR * 128], F32, kind="ExternalInput").ap()
    onescin = nc.dram_tensor("onesc", [128, 1], F32, kind="ExternalInput").ap()
    onesrin = nc.dram_tensor("onesr", [1, 128], F32, kind="ExternalInput").ap()
    msendin = nc.dram_tensor("msend", [128, 1], F32, kind="ExternalInput").ap()
    mrecvin = nc.dram_tensor("mrecv", [128, 1], F32, kind="ExternalInput").ap()

    lam = [2.0 ** (-8.0 * (h + 1) / 16.0) for h in range(H)]
    blk = [math.exp(-CH * lam[h]) for h in range(H)]

    es = contextlib.ExitStack()
    ec = es.enter_context
    with es:
        xsb = ec(nc.sbuf_tensor("xsb", [128, DCH * FT], F32))
        xn = ec(nc.sbuf_tensor("xn", [128, DCH * FT], BF16))
        hq = ec(nc.sbuf_tensor("hq", [128, FCH * FT], BF16))
        kfm = ec(nc.sbuf_tensor("kfm", [128, HKV * AT], BF16))
        vfm = ec(nc.sbuf_tensor("vfm", [128, HKV * AT], BF16))
        ktok = [ec(nc.sbuf_tensor(f"ktok{i}", [CH, HKV * DH], BF16)) for i in range(2)]
        vtok = [ec(nc.sbuf_tensor(f"vtok{i}", [CH, HKV * DH], BF16)) for i in range(2)]
        kvsb = ec(nc.sbuf_tensor("kvsb", [128, H * DH], F32))
        kvbf = [ec(nc.sbuf_tensor(f"kvbf{i}", [128, DH], BF16)) for i in range(2)]
        qtil = [ec(nc.sbuf_tensor(f"qtil{i}", [128, CH], BF16)) for i in range(2)]
        ktil = [ec(nc.sbuf_tensor(f"ktil{i}", [CH, DH], BF16)) for i in range(2)]
        scbf = [ec(nc.sbuf_tensor(f"scbf{i}", [CH, CH], BF16)) for i in range(2)]
        wb = [ec(nc.sbuf_tensor(f"wb{i}", [128, 2048], BF16)) for i in range(3)]
        gbf = ec(nc.sbuf_tensor("gbf16", [128, FT], BF16))
        scr = [ec(nc.sbuf_tensor(f"scr{i}", [128, 512], F32)) for i in range(4)]
        rows = ec(nc.sbuf_tensor("rows", [1, 4 * 512], F32))
        gT = ec(nc.sbuf_tensor("gT", [NR, FT], BF16))
        gtok_o = ec(nc.sbuf_tensor("gtok_o", [128, NR], F32))
        gtok_w = ec(nc.sbuf_tensor("gtok_w", [128, NR], F32))
        gtmp = ec(nc.sbuf_tensor("gtmp", [128, NR], F32))
        gm = ec(nc.sbuf_tensor("gm", [128, 8], F32))
        gbf16t = ec(nc.sbuf_tensor("gbf16t", [128, NR], BF16))
        onesc = ec(nc.sbuf_tensor("onesc_s", [128, 1], F32))
        onesr = ec(nc.sbuf_tensor("onesr_s", [1, 128], F32))
        eye_b = ec(nc.sbuf_tensor("eye_b", [128, 128], BF16))
        eye_f = ec(nc.sbuf_tensor("eye_f", [128, 128], F32))
        rot_b = ec(nc.sbuf_tensor("rot_b", [128, 128], BF16))
        b1sb = ec(nc.sbuf_tensor("b1sb", [128, (NR + NS) * FCH], F32))
        b2r_s = ec(nc.sbuf_tensor("b2r_s", [NR, D], BF16))
        b2s_s = ec(nc.sbuf_tensor("b2s_s", [NS, D], BF16))
        ones2 = ec(nc.sbuf_tensor("ones2", [NS, 512], BF16))
        esel = ec(nc.sbuf_tensor("esel_s", [NR, NR * 128], BF16))
        cent = ec(nc.sbuf_tensor("cent_s", [128, DCH * NR], BF16))
        biasr = ec(nc.sbuf_tensor("biasr", [NR, 1], F32))
        kdec = ec(nc.sbuf_tensor("kdec_s", [CH, H], F32))
        diag = ec(nc.sbuf_tensor("diag_s", [CH, H * CH], BF16))
        qdec = ec(nc.sbuf_tensor("qdec_s", [128, H * CH], BF16))
        msend = ec(nc.sbuf_tensor("msend_s", [128, 1], F32))
        mrecv = ec(nc.sbuf_tensor("mrecv_s", [128, 1], F32))
        costl = ec(nc.sbuf_tensor("costl", [128, 512], F32))
        epsc = ec(nc.sbuf_tensor("epsc", [1, 1], F32))
        sintl = ec(nc.sbuf_tensor("sintl", [128, 512], F32))
        pp = [ec(nc.psum_tensor(f"pp{i}", [128, 512], F32)) for i in range(6)]
        pb = [ec(nc.psum_tensor(f"pbf{i}", [128, 1024], BF16)) for i in range(2)]

        sems = {e: ec(nc.semaphore(f"sem_{e}")) for e in ENGINES}

        P = Prog()

        bx = [Buf() for _ in range(DCH)]
        bxn = [Buf() for _ in range(DCH)]
        bh = [Buf() for _ in range(FCH)]
        bkfm = [Buf() for _ in range(HKV)]
        bvfm = [Buf() for _ in range(HKV)]
        bktok = [Buf(), Buf()]
        bvtok = [Buf(), Buf()]
        bkv = [Buf() for _ in range(H)]
        bkvbf = [Buf(), Buf()]
        bqtil = [Buf(), Buf()]
        bktil = [Buf(), Buf()]
        bscbf = [Buf(), Buf()]
        bwb = [Buf() for _ in range(3)]
        bgbf = Buf()
        bscr = [Buf() for _ in range(4)]
        brows = [Buf() for _ in range(4)]
        bgT = [Buf(), Buf()]
        bgtok_o = Buf(); bgtok_w = Buf(); bgtmp = Buf()
        bgm = [Buf() for _ in range(8)]
        bgbf16t = Buf()
        bconst = Buf()
        bpp = [Buf(psum=True) for _ in range(6)]
        bpb = [Buf(psum=True), Buf(psum=True)]
        bcos = Buf(); bsin = Buf()
        bx1 = [Buf() for _ in range(4)]
        bx2 = [Buf() for _ in range(4)]
        bccin = Buf(); bccout = Buf()
        bout = [Buf() for _ in range(4)]

        wb_ctr = [0]

        def next_wb():
            i = wb_ctr[0] % 3
            wb_ctr[0] += 1
            return i

        def ld(engine, dst_ap, src_ap, buf):
            P.op(engine, lambda eng, d=dst_ap, s=src_ap: eng.dma_start(d, s),
                 reads=(), writes=(buf,), inc=16)

        ld("sync", onesc[:], onescin[:], bconst)
        ld("sync", onesr[:], onesrin[:], bconst)
        ld("sync", eye_f[:], eyein[:], bconst)
        ld("sync", b1sb[:], b1in[:], bconst)
        ld("sync", biasr[:], biasin[:], bconst)
        ld("sync", kdec[:], kdecin[:], bconst)
        ld("sync", msend[:], msendin[:], bconst)
        ld("sync", mrecv[:], mrecvin[:], bconst)
        ld("pool", eye_b[:], eyein[:], bconst)
        ld("pool", rot_b[:], rotin[:], bconst)
        ld("pool", b2r_s[:], rb2[:], bconst)
        ld("pool", b2s_s[:], sb2[:], bconst)
        ld("pool", esel[:], eselin[:], bconst)
        ld("pool", cent[:].rearrange("p (c e) -> p c e", e=NR),
           centin.rearrange("c p e -> p c e"), bconst)
        ld("pool", diag[:], diagin[:], bconst)
        ld("pool", qdec[:], qdecin[:], bconst)
        P.op("dve", lambda eng: eng.memset(ones2[:], 1.0), writes=(bconst,))
        P.op("dve", lambda eng: eng.memset(epsc[:], 1e-5), writes=(bconst,))

        def emit_ln(xsrc3, xdst_bufs_dram, tcols, width, update_x):
            nsub = width // 512
            for sc in range(nsub):
                c0 = tcols + sc * 512
                ps1 = pp[4]; ps2 = pp[5]
                for ch in range(DCH):
                    xap = xsb[:, ch * FT + c0: ch * FT + c0 + 512]
                    P.op("pe", lambda eng, o=ps1[0:1, :], a=onesc[:], b=xap,
                         s=(ch == 0), t=(ch == DCH - 1):
                         eng.matmul(o, a, b, start=s, stop=t),
                         reads=(bconst, bx[ch]), writes=(bpp[4],))
                    sq = scr[ch % 2]
                    P.op("act", lambda eng, o=sq[:], i=xap:
                         eng.activation(o, i, AF.Square),
                         reads=(bx[ch],), writes=(bscr[ch % 2],))
                    P.op("pe", lambda eng, o=ps2[0:1, :], a=onesc[:], b=sq[:],
                         s=(ch == 0), t=(ch == DCH - 1):
                         eng.matmul(o, a, b, start=s, stop=t),
                         reads=(bconst, bscr[ch % 2]), writes=(bpp[5],))
                r0 = rows[:, 0:512]; r1 = rows[:, 512:1024]
                r2 = rows[:, 1024:1536]; r3 = rows[:, 1536:2048]
                P.op("dve", lambda eng: eng.tensor_copy(r0, ps1[0:1, :]),
                     reads=(bpp[4],), writes=(brows[0],))
                P.op("dve", lambda eng: eng.tensor_copy(r1, ps2[0:1, :]),
                     reads=(bpp[5],), writes=(brows[1],))
                inv = 1.0 / D
                P.op("dve", lambda eng: eng.tensor_scalar_mul(r0, r0, inv),
                     reads=(brows[0],), writes=(brows[0],))
                P.op("dve", lambda eng: eng.tensor_scalar_mul(r1, r1, inv),
                     reads=(brows[1],), writes=(brows[1],))
                P.op("dve", lambda eng: eng.tensor_mul(r2, r0, r0),
                     reads=(brows[0],), writes=(brows[2],))
                P.op("dve", lambda eng: eng.tensor_sub(r2, r1, r2),
                     reads=(brows[1], brows[2]), writes=(brows[2],))
                P.op("act", lambda eng: eng.activation(r2, r2, AF.Sqrt,
                                                       bias=epsc[0:1, 0:1]),
                     reads=(brows[2], bconst), writes=(brows[2],))
                P.op("dve", lambda eng: eng.reciprocal(r2, r2),
                     reads=(brows[2],), writes=(brows[2],))
                P.op("dve", lambda eng: eng.scalar_tensor_tensor(
                    r3, r2, -1.0, r0, ALU.mult, ALU.mult),
                    reads=(brows[2], brows[0]), writes=(brows[3],))
                psA = pp[4]; psB = pp[5]
                P.op("pe", lambda eng: eng.matmul(psA[:, :], onesr[:], r2,
                                                  start=True, stop=True),
                     reads=(bconst, brows[2]), writes=(bpp[4],))
                P.op("pe", lambda eng: eng.matmul(psB[:, :], onesr[:], r3,
                                                  start=True, stop=True),
                     reads=(bconst, brows[3]), writes=(bpp[5],))
                for ch in range(DCH):
                    xap = xsb[:, ch * FT + c0: ch * FT + c0 + 512]
                    xnap = xn[:, ch * FT + c0: ch * FT + c0 + 512]
                    t = scr[2 + ch % 2]
                    P.op("dve", lambda eng, t=t[:], x=xap: eng.tensor_mul(
                        t, x, psA[:, :]),
                        reads=(bx[ch], bpp[4]), writes=(bscr[2 + ch % 2],))
                    P.op("dve", lambda eng, t=t[:], o=xnap: eng.tensor_add(
                        o, t, psB[:, :]),
                        reads=(bscr[2 + ch % 2], bpp[5]), writes=(bxn[ch],))
                    if update_x:
                        P.op("dve", lambda eng, x=xap, o=xnap:
                             eng.scalar_tensor_tensor(x, o, 0.5, x,
                                                      ALU.mult, ALU.add),
                             reads=(bxn[ch], bx[ch]), writes=(bx[ch],))

        def emit_gating(tcols):
            for sc in range(2):
                c0 = tcols + sc * 512
                pss = pp[4]
                for ch in range(DCH):
                    P.op("pe", lambda eng, a=cent[:, ch * NR:(ch + 1) * NR],
                         b=xn[:, ch * FT + c0: ch * FT + c0 + 512],
                         s=(ch == 0), t=(ch == DCH - 1):
                         eng.matmul(pss[0:NR, :], a, b, start=s, stop=t),
                         reads=(bconst, bxn[ch]), writes=(bpp[4],))
                sc_scores = scr[3][0:NR, :]
                P.op("act", lambda eng: eng.activation(sc_scores, pss[0:NR, :],
                                                       AF.Sigmoid),
                     reads=(bpp[4],), writes=(bscr[3],))
                P.op("dve", lambda eng: eng.tensor_scalar_add(
                    sc_scores, sc_scores, biasr[:, 0:1]),
                    reads=(bscr[3], bconst), writes=(bscr[3],))
                for mc in range(4):
                    m0 = mc * 128
                    pst = pp[5]
                    P.op("pe", lambda eng, i=scr[3][0:NR, m0:m0 + 128]:
                         eng.transpose(pst[:, 0:NR], i, eye_f[0:NR, 0:NR]),
                         reads=(bscr[3], bconst), writes=(bpp[5],))
                    P.op("dve", lambda eng: eng.tensor_copy(gtok_o[:], pst[:, 0:NR]),
                         reads=(bpp[5],), writes=(bgtok_o,))
                    P.op("dve", lambda eng: eng.tensor_copy(gtok_w[:], gtok_o[:]),
                         reads=(bgtok_o,), writes=(bgtok_w,))
                    for r in range(KR):
                        P.op("dve", lambda eng, m=gm[:, r:r + 1]:
                             eng.reduce_max(m, gtok_w[:], axis=AX.X),
                             reads=(bgtok_w,), writes=(bgm[r],))
                        if r < KR - 1:
                            P.op("dve", lambda eng, m=gm[:, r:r + 1]:
                                 eng.tensor_scalar(gtmp[:], gtok_w[:], m, None,
                                                   ALU.is_ge),
                                 reads=(bgtok_w, bgm[r]), writes=(bgtmp,))
                            P.op("dve", lambda eng: eng.scalar_tensor_tensor(
                                gtok_w[:], gtmp[:], -1e30, gtok_w[:],
                                ALU.mult, ALU.add),
                                reads=(bgtmp, bgtok_w), writes=(bgtok_w,))
                    P.op("dve", lambda eng: eng.tensor_scalar(
                        gtmp[:], gtok_o[:], gm[:, 3:4], None, ALU.is_ge),
                        reads=(bgtok_o, bgm[3]), writes=(bgtmp,))
                    P.op("dve", lambda eng: eng.tensor_mul(
                        gtok_o[:], gtok_o[:], gtmp[:]),
                        reads=(bgtok_o, bgtmp), writes=(bgtok_o,))
                    P.op("dve", lambda eng: eng.reduce_sum(
                        gm[:, 4:5], gtok_o[:], axis=AX.X),
                        reads=(bgtok_o,), writes=(bgm[4],))
                    P.op("dve", lambda eng: eng.tensor_scalar_max(
                        gm[:, 4:5], gm[:, 4:5], 1e-8),
                        reads=(bgm[4],), writes=(bgm[4],))
                    P.op("dve", lambda eng: eng.reciprocal(gm[:, 5:6], gm[:, 4:5]),
                         reads=(bgm[4],), writes=(bgm[5],))
                    P.op("dve", lambda eng: eng.tensor_scalar(
                        gbf16t[:], gtok_o[:], gm[:, 5:6], None, ALU.mult),
                        reads=(bgtok_o, bgm[5]), writes=(bgbf16t,))
                    P.op("pe", lambda eng: eng.transpose(
                        pb[0][0:NR, 0:128], gbf16t[:], eye_b[:]),
                        reads=(bgbf16t, bconst), writes=(bpb[0],))
                    P.op("act", lambda eng, o=gT[:, sc * 512 + m0: sc * 512 + m0 + 128]:
                         eng.copy(o, pb[0][0:NR, 0:128]),
                         reads=(bpb[0],), writes=(bgT[sc],))

        def emit_expert(kind, j, tcols):
            w1ap = (sw1 if kind == "s" else rw1)
            w2ap = (sw2 if kind == "s" else rw2)
            bcol0 = (NR * FCH + j * FCH) if kind == "s" else j * FCH
            if kind == "r":
                for sc in range(2):
                    psg = pp[2 + sc]
                    P.op("pe", lambda eng, a=esel[:, j * 128:(j + 1) * 128],
                         g=gT[:, sc * 512:(sc + 1) * 512], o=psg[:, :]:
                         eng.matmul(o, a, g, start=True, stop=True),
                         reads=(bconst, bgT[sc]), writes=(bpp[2 + sc],))
                    P.op("act", lambda eng, o=gbf[:, sc * 512:(sc + 1) * 512],
                         i=psg[:, :]: eng.copy(o, i),
                         reads=(bpp[2 + sc],), writes=(bgbf,))
            for fb in range(FCH):
                wi = next_wb()
                src = w1ap[j, fb].rearrange("c p f -> p c f")
                P.op("pool", lambda eng,
                     d=wb[wi][:, 0:2048].rearrange("p (c f) -> p c f", f=128), s=src:
                     eng.dma_start(d, s), reads=(), writes=(bwb[wi],), inc=16)
                for sc in range(2):
                    psh = pp[sc]
                    c0 = tcols + sc * 512
                    for ch in range(DCH):
                        P.op("pe", lambda eng, a=wb[wi][:, ch * 128:(ch + 1) * 128],
                             b=xn[:, ch * FT + c0: ch * FT + c0 + 512],
                             o=psh[:, :], s=(ch == 0), t=(ch == DCH - 1):
                             eng.matmul(o, a, b, start=s, stop=t),
                             reads=(bwb[wi], bxn[ch]), writes=(bpp[sc],))
                    hap = hq[:, fb * FT + sc * 512: fb * FT + sc * 512 + 512]
                    P.op("act", lambda eng, o=hap, i=psh[:, :],
                         b=b1sb[:, bcol0 + fb: bcol0 + fb + 1]:
                         eng.activation(o, i, AF.Relu, bias=b),
                         reads=(bpp[sc], bconst), writes=(bh[fb],))
                    if kind == "r":
                        P.op("dve", lambda eng, o=hap,
                             g=gbf[:, sc * 512:(sc + 1) * 512]:
                             eng.tensor_mul(o, o, g),
                             reads=(bh[fb], bgbf), writes=(bh[fb],))
            for db in range(DCH):
                wi = next_wb()
                src = w2ap[j, db].rearrange("c p f -> p c f")
                P.op("pool", lambda eng,
                     d=wb[wi][:, 0:FCH * 128].rearrange("p (c f) -> p c f", f=128), s=src:
                     eng.dma_start(d, s), reads=(), writes=(bwb[wi],), inc=16)
                for sc in range(2):
                    psy = pp[2 + (db % 2) * 2 + sc]
                    bpsy = bpp[2 + (db % 2) * 2 + sc]
                    c0 = tcols + sc * 512
                    first = True
                    if kind == "s" and j == 0:
                        P.op("pe", lambda eng, a=b2s_s[:, db * 128:(db + 1) * 128],
                             o=psy[:, :]: eng.matmul(o, a, ones2[:, :],
                                                     start=True, stop=False),
                             reads=(bconst,), writes=(bpsy,))
                        P.op("pe", lambda eng, a=b2r_s[:, db * 128:(db + 1) * 128],
                             g=gT[:, sc * 512:(sc + 1) * 512], o=psy[:, :]:
                             eng.matmul(o, a, g, start=False, stop=False),
                             reads=(bconst, bgT[sc]), writes=(bpsy,))
                        first = False
                    for fb in range(FCH):
                        hap = hq[:, fb * FT + sc * 512: fb * FT + sc * 512 + 512]
                        P.op("pe", lambda eng, a=wb[wi][:, fb * 128:(fb + 1) * 128],
                             b=hap, o=psy[:, :], s=(first and fb == 0),
                             t=(fb == FCH - 1):
                             eng.matmul(o, a, b, start=s, stop=t),
                             reads=(bwb[wi], bh[fb]), writes=(bpsy,))
                    xap = xsb[:, db * FT + c0: db * FT + c0 + 512]
                    P.op("dve", lambda eng, x=xap, p=psy[:, :]:
                         eng.scalar_tensor_tensor(x, p, 0.5, x, ALU.mult, ALU.add),
                         reads=(bpsy, bx[db]), writes=(bx[db],))

        def load_x_tile(src, tcols, width):
            src3 = src.rearrange("(c p) t -> p c t", p=128)
            dst3 = xsb[:, :].rearrange("p (c t) -> p c t", t=FT)
            return (dst3[:, :, 0:width], src3[:, :, tcols:tcols + width])

        def store_x_tile(dst, tcols, width):
            src3 = xsb[:, :].rearrange("p (c t) -> p c t", t=FT)
            dst3 = dst.rearrange("(c p) t -> p c t", p=128)
            return (dst3[:, :, tcols:tcols + width], src3[:, :, 0:width])

        def emit_ffn(src, dst, src_bufs, dst_bufs):
            for tile in range(2):
                tc = tile * FT
                d, s = load_x_tile(src, tc, FT)
                P.op("sync", lambda eng, d=d, s=s: eng.dma_start(d, s),
                     reads=tuple(src_bufs[2 * tile: 2 * tile + 2]),
                     writes=tuple(bx), inc=16)
                emit_ln(None, None, 0, FT, update_x=True)
                emit_gating(0)
                for (kind, j) in [("s", 0), ("s", 1)] + [("r", jj) for jj in range(NR)]:
                    emit_expert(kind, j, 0)
                d, s = store_x_tile(dst, tc, FT)
                P.op("sync", lambda eng, d=d, s=s: eng.dma_start(d, s),
                     reads=tuple(bx),
                     writes=tuple(dst_bufs[2 * tile: 2 * tile + 2]), inc=16)

        def proj_rope(w_ap, nheads, dst_sb, dst_bufs, do_rope, at):
            for hh in range(nheads):
                wi = next_wb()
                src = w_ap[hh].rearrange("c p f -> p c f")
                P.op("pool", lambda eng,
                     d=wb[wi][:, 0:2048].rearrange("p (c f) -> p c f", f=128), s=src:
                     eng.dma_start(d, s), reads=(), writes=(bwb[wi],), inc=16)
                psp = pp[hh % 2]
                for ch in range(DCH):
                    P.op("pe", lambda eng, a=wb[wi][:, ch * 128:(ch + 1) * 128],
                         b=xn[:, ch * FT: ch * FT + 512], o=psp[:, :],
                         s=(ch == 0), t=(ch == DCH - 1):
                         eng.matmul(o, a, b, start=s, stop=t),
                         reads=(bwb[wi], bxn[ch]), writes=(bpp[hh % 2],))
                if do_rope:
                    kraw = hq[:, 8 * FT: 8 * FT + 512]
                    t0 = scr[hh % 2]; t1 = scr[2 + hh % 2]
                    P.op("dve", lambda eng, t=t0[:], p=psp[:, :]:
                         eng.tensor_mul(t, p, costl[:]),
                         reads=(bpp[hh % 2], bcos), writes=(bscr[hh % 2],))
                    if ATTN_ROPE >= 2:
                        P.op("act", lambda eng, i=psp[:, :]: eng.copy(kraw, i),
                             reads=(bpp[hh % 2],), writes=(bh[8],))
                    if ATTN_ROPE >= 3:
                        psr = pp[2 + hh % 2]
                        P.op("pe", lambda eng, o=psr[:, :]:
                             eng.matmul(o, rot_b[:], kraw, start=True, stop=True),
                             reads=(bconst, bh[8]), writes=(bpp[2 + hh % 2],))
                    if ATTN_ROPE >= 4:
                        P.op("dve", lambda eng, t=t1[:], p=psr[:, :]:
                             eng.tensor_mul(t, p, sintl[:]),
                             reads=(bpp[2 + hh % 2], bsin), writes=(bscr[2 + hh % 2],))
                    if ATTN_ROPE >= 5:
                        P.op("dve", lambda eng, o=dst_sb[:, hh * 512:(hh + 1) * 512],
                             a=t0[:], b=t1[:]: eng.tensor_add(o, a, b),
                             reads=(bscr[hh % 2], bscr[2 + hh % 2]),
                             writes=tuple(dst_bufs(hh)))
                    else:
                        P.op("dve", lambda eng, o=dst_sb[:, hh * 512:(hh + 1) * 512],
                             a=t0[:]: eng.tensor_copy(o, a),
                             reads=(bscr[hh % 2],),
                             writes=tuple(dst_bufs(hh)))
                else:
                    P.op("act", lambda eng, o=dst_sb[:, hh * 512:(hh + 1) * 512],
                         i=psp[:, :]: eng.copy(o, i),
                         reads=(bpp[hh % 2],), writes=tuple(dst_bufs(hh)))

        def emit_attn_phase(phase, at, bqo=None):
            tc = at * AT
            d, s = load_x_tile(x1 if not mode.startswith('attn') else xin, tc, AT)
            P.op("sync", lambda eng, d=d, s=s: eng.dma_start(d, s),
                 reads=(bx1[at],) if not mode.startswith('attn') else (),
                 writes=tuple(bx), inc=16)
            emit_ln(None, None, 0, AT, update_x=False)
            P.op("sync", lambda eng: eng.dma_start(costl[:], cosin[:, tc:tc + AT]),
                 reads=(bcos,), writes=(bcos,), inc=16)
            P.op("sync", lambda eng: eng.dma_start(sintl[:], sinin[:, tc:tc + AT]),
                 reads=(bsin,), writes=(bsin,), inc=16)
            if ATTN_DBG >= 2:
                rope_on = "r" in ATTN_SUB
                if "k" in ATTN_SUB:
                    proj_rope(wk_in, HKV, kfm, lambda h: (bkfm[h],), rope_on, at)
                if "v" in ATTN_SUB:
                    proj_rope(wv_in, HKV, vfm, lambda h: (bvfm[h],), False, at)
                if phase == "B" and "q" in ATTN_SUB:
                    proj_rope(wq_in, H, hq,
                              lambda h: tuple(bqo[h * 8 + c] for c in range(8)), rope_on, at)
            for c in range(8 if ATTN_DBG >= 3 else 0):
                pg = c % 2
                for kvh in range(HKV):
                    P.op("pe", lambda eng, i=kfm[:, kvh * AT + c * CH: kvh * AT + c * CH + CH],
                         o=pb[0][0:CH, kvh * 128:(kvh + 1) * 128]:
                         eng.transpose(o, i, eye_b[:]),
                         reads=(bkfm[kvh], bconst), writes=(bpb[0],))
                    P.op("pe", lambda eng, i=vfm[:, kvh * AT + c * CH: kvh * AT + c * CH + CH],
                         o=pb[1][0:CH, kvh * 128:(kvh + 1) * 128]:
                         eng.transpose(o, i, eye_b[:]),
                         reads=(bvfm[kvh], bconst), writes=(bpb[1],))
                P.op("act", lambda eng, o=ktok[pg][:, :], i=pb[0][0:CH, 0:HKV * 128]:
                     eng.copy(o, i), reads=(bpb[0],), writes=(bktok[pg],))
                P.op("act", lambda eng, o=vtok[pg][:, :], i=pb[1][0:CH, 0:HKV * 128]:
                     eng.copy(o, i), reads=(bpb[1],), writes=(bvtok[pg],))
                for qh in range(H if ATTN_DBG >= 4 else 0):
                    kvh = qh // 4
                    pq = qh % 2
                    if phase == "B" and ATTN_QH >= 1:
                        pss = pp[pq]
                        P.op("pe", lambda eng, a=kfm[:, kvh * AT + c * CH: kvh * AT + c * CH + CH],
                             b=hq[:, qh * 512 + c * CH: qh * 512 + c * CH + CH],
                             o=pss[0:CH, 0:CH]:
                             eng.matmul(o, a, b, start=True, stop=True),
                             reads=(bkfm[kvh], bqo[qh * 8 + c]), writes=(bpp[pq],))
                        P.op("dve", lambda eng, o=scbf[pq][:, :], p=pss[0:CH, 0:CH],
                             dg=diag[:, qh * CH:(qh + 1) * CH]:
                             eng.tensor_mul(o, p, dg),
                             reads=(bpp[pq], bconst), writes=(bscbf[pq],))
                        P.op("act", lambda eng, o=kvbf[pq][:, :],
                             i=kvsb[:, qh * DH:(qh + 1) * DH]: eng.copy(o, i),
                             reads=(bkv[qh],), writes=(bkvbf[pq],))
                        P.op("dve", lambda eng, o=qtil[pq][:, :],
                             q=hq[:, qh * 512 + c * CH: qh * 512 + c * CH + CH],
                             qd=qdec[:, qh * CH:(qh + 1) * CH]:
                             eng.tensor_mul(o, q, qd),
                             reads=(bqo[qh * 8 + c], bconst), writes=(bqtil[pq],))
                        pso = pp[2 + pq]
                        if ATTN_QH >= 2:
                            P.op("pe", lambda eng, a=vtok[pg][:, kvh * 128:(kvh + 1) * 128],
                                 b=scbf[pq][:, :], o=pso[:, 0:CH],
                                 t=(ATTN_QH < 3):
                                 eng.matmul(o, a, b, start=True, stop=t),
                                 reads=(bvtok[pg], bscbf[pq]), writes=(bpp[2 + pq],))
                        if ATTN_QH >= 3:
                            P.op("pe", lambda eng, a=kvbf[pq][:, :], b=qtil[pq][:, :],
                                 o=pso[:, 0:CH]:
                                 eng.matmul(o, a, b, start=False, stop=True),
                                 reads=(bkvbf[pq], bqtil[pq]), writes=(bpp[2 + pq],))
                        if ATTN_QH >= 2:
                            P.op("act", lambda eng, o=hq[:, qh * 512 + c * CH: qh * 512 + c * CH + CH],
                                 i=pso[:, 0:CH]: eng.copy(o, i),
                                 reads=(bpp[2 + pq],), writes=(bqo[qh * 8 + c],))
                    if ATTN_QH < 4 or phase == "skip":
                        continue
                    P.op("dve", lambda eng, o=ktil[pq][:, :],
                         k=ktok[pg][:, kvh * 128:(kvh + 1) * 128],
                         kd=kdec[:, qh:qh + 1]:
                         eng.tensor_scalar(o, k, kd, None, ALU.mult),
                         reads=(bktok[pg], bconst), writes=(bktil[pq],))
                    psu = pp[4 + pq]
                    P.op("pe", lambda eng, a=ktil[pq][:, :],
                         b=vtok[pg][:, kvh * 128:(kvh + 1) * 128], o=psu[:, 0:DH]:
                         eng.matmul(o, a, b, start=True, stop=True),
                         reads=(bktil[pq], bvtok[pg]), writes=(bpp[4 + pq],))
                    P.op("dve", lambda eng, kv=kvsb[:, qh * DH:(qh + 1) * DH],
                         p=psu[:, 0:DH], bl=blk[qh]:
                         eng.scalar_tensor_tensor(kv, kv, bl, p, ALU.mult, ALU.add),
                         reads=(bkv[qh], bpp[4 + pq]), writes=(bkv[qh],))
            if phase == "B" and ATTN_DBG >= 2 and "o" in ATTN_SUB:
                for db in range(DCH):
                    wi = next_wb()
                    src = wo_in[db].rearrange("c p f -> p c f")
                    P.op("pool", lambda eng,
                         d=wb[wi][:, 0:2048].rearrange("p (c f) -> p c f", f=128), s=src:
                         eng.dma_start(d, s), reads=(), writes=(bwb[wi],), inc=16)
                    psy = pp[4 + db % 2]
                    for eb in range(H):
                        P.op("pe", lambda eng, a=wb[wi][:, eb * 128:(eb + 1) * 128],
                             b=hq[:, eb * 512: eb * 512 + 512], o=psy[:, :],
                             s=(eb == 0), t=(eb == H - 1):
                             eng.matmul(o, a, b, start=s, stop=t),
                             reads=tuple([bwb[wi]] + [bqo[eb * 8 + cc] for cc in range(8)]),
                             writes=(bpp[4 + db % 2],))
                    xap = xsb[:, db * FT: db * FT + 512]
                    P.op("dve", lambda eng, x=xap, p=psy[:, :]:
                         eng.tensor_add(x, x, p),
                         reads=(bpp[4 + db % 2], bx[db]), writes=(bx[db],))
            if phase == "B":
                dst = x2 if not mode.startswith('attn') else yout
                dbufs = bx2 if not mode.startswith('attn') else bout
                d, s = store_x_tile(dst, tc, AT)
                P.op("sync", lambda eng, d=d, s=s: eng.dma_start(d, s),
                     reads=tuple(bx), writes=(dbufs[at],), inc=16)

        def emit_attn():
            P.op("dve", lambda eng: eng.memset(kvsb[:, :], 0.0),
                 writes=tuple(bkv))
            for at in range(4):
                emit_attn_phase("A", at)
            P.op("dve", lambda eng: eng.tensor_scalar(
                kvsb[:, :], kvsb[:, :], msend[:, 0:1], None, ALU.mult),
                reads=tuple(bkv) + (bconst,), writes=tuple(bkv))
            if mode != "attn_nocc":
                P.op("sync", lambda eng: eng.dma_start(cc_in, kvsb[:, :]),
                     reads=tuple(bkv), writes=(bccin,), inc=16)
                P.op("cc", lambda eng: eng.collective_compute(
                    "AllReduce", ALU.add,
                    replica_groups=[[0, 1], [2, 3], [4, 5], [6, 7]],
                    ins=[cc_in], outs=[cc_out]),
                    reads=(bccin,), writes=(bccout,), inc=1)
                P.op("sync", lambda eng: eng.dma_start(kvsb[:, :], cc_out),
                     reads=(bccout,) + tuple(bkv), writes=tuple(bkv), inc=16)
            P.op("dve", lambda eng: eng.tensor_scalar(
                kvsb[:, :], kvsb[:, :], mrecv[:, 0:1], None, ALU.mult),
                reads=tuple(bkv) + (bconst,), writes=tuple(bkv))
            for at in range(4):
                bqo = [rebind(P, bh) for _ in range(H * 8)]
                emit_attn_phase("B", at, bqo=bqo)
                newh = [rebind(P, bqo) for _ in range(FCH)]
                for i in range(FCH):
                    bh[i] = newh[i]

        if mode == "full":
            emit_ffn(xin, x1, [Buf() for _ in range(4)], bx1)
            emit_attn()
            emit_ffn(x2, yout, bx2, bout)
        elif mode == "ffn1":
            emit_ffn(xin, yout, [Buf() for _ in range(4)], bout)
        elif mode in ("attn", "attn_nocc"):
            emit_attn()
        P.op("sync", lambda eng: eng.nop(), reads=tuple(bout))

        P.replay(nc, sems)
    return nc



def _arr(a):
    return np.ascontiguousarray(a, dtype=np.float32)


def _prep_shared(inp):
    rw1 = inp["L0_ffn_routed_w1"]
    sw1 = inp["L0_ffn_shared_w1"]
    rw2 = inp["L0_ffn_routed_w2"]
    sw2 = inp["L0_ffn_shared_w2"]

    def w1_arr(w):
        n = w.shape[0]
        return _arr(np.transpose(w.reshape(n, DCH, 128, FCH, 128), (0, 3, 1, 2, 4)))

    def w2_arr(w):
        n = w.shape[0]
        return _arr(np.transpose(w.reshape(n, FCH, 128, DCH, 128), (0, 3, 1, 2, 4)))

    b1a = np.concatenate([np.asarray(inp["L0_ffn_routed_b1"]),
                          np.asarray(inp["L0_ffn_shared_b1"])], axis=0)
    b1a = _arr(np.transpose(b1a.reshape(NR + NS, FCH, 128), (2, 0, 1))
               .reshape(128, (NR + NS) * FCH))

    cent = np.asarray(inp["L0_ffn_centroids"])
    centT = _arr(cent.T.reshape(DCH, 128, NR))

    wq = np.asarray(inp["L0_attn_wq"])
    wk = np.asarray(inp["L0_attn_wk"])
    wv = np.asarray(inp["L0_attn_wv"])
    wo = np.asarray(inp["L0_attn_wo"])

    def wqT_arr(w, nh):
        wT = w.T
        return _arr(np.transpose(wT.reshape(DCH, 128, nh, 128), (2, 0, 1, 3)))

    woT = wo.T
    woT_arr = _arr(np.transpose(woT.reshape(H, 128, DCH, 128), (2, 0, 1, 3)))

    hs = np.arange(1, H + 1, dtype=np.float64)
    lam = np.exp2(-8.0 * hs / H)
    r = np.arange(CH, dtype=np.float64)
    diff = r[:, None] - r[None, :]
    diagT = np.zeros((CH, H * CH), dtype=np.float64)
    for h in range(H):
        m = np.where(diff >= 0, np.exp(-lam[h] * diff), 0.0)
        diagT[:, h * CH:(h + 1) * CH] = m.T
    qdec = np.zeros((128, H * CH), dtype=np.float64)
    for h in range(H):
        qdec[:, h * CH:(h + 1) * CH] = np.exp(-lam[h] * (r + 1.0))[None, :]
    kdec = np.zeros((CH, H), dtype=np.float64)
    for h in range(H):
        kdec[:, h] = np.exp(-lam[h] * (CH - 1.0 - r))

    eye = np.eye(128, dtype=np.float32)
    rot = np.zeros((128, 128), dtype=np.float32)
    for m in range(64):
        rot[m + 64, m] = -1.0
    for m in range(64, 128):
        rot[m - 64, m] = 1.0
    esel = np.zeros((NR, NR * 128), dtype=np.float32)
    for j in range(NR):
        esel[j, j * 128:(j + 1) * 128] = 1.0

    return dict(
        rw1=w1_arr(rw1), sw1=w1_arr(sw1), rw2=w2_arr(rw2), sw2=w2_arr(sw2),
        b1a=b1a, rb2=_arr(inp["L0_ffn_routed_b2"]), sb2=_arr(inp["L0_ffn_shared_b2"]),
        centT=centT, rbias=_arr(np.asarray(inp["L0_ffn_bias"]).reshape(NR, 1)),
        wqT=wqT_arr(wq, H), wkT=wqT_arr(wk, HKV), wvT=wqT_arr(wv, HKV),
        woT=woT_arr,
        diagT=_arr(diagT), qdec=_arr(qdec), kdec=_arr(kdec),
        eye=eye, rotR=rot, esel=esel,
        onesc=np.ones((128, 1), np.float32), onesr=np.ones((1, 128), np.float32),
    )


def _rope_tables(p0):
    d = DH
    inv = 1.0 / (10000.0 ** (np.arange(0, d, 2, dtype=np.float64) / d))
    t = np.arange(p0, p0 + NTOK, dtype=np.float64)
    fr = t[None, :] * np.concatenate([inv, inv])[:, None]
    return _arr(np.cos(fr)), _arr(np.sin(fr))


def kernel(L0_attn_x, L0_attn_wq, L0_attn_wk, L0_attn_wv, L0_attn_wo,
           L0_ffn_shared_w1, L0_ffn_shared_b1, L0_ffn_shared_w2, L0_ffn_shared_b2,
           L0_ffn_routed_w1, L0_ffn_routed_b1, L0_ffn_routed_w2, L0_ffn_routed_b2,
           L0_ffn_centroids, L0_ffn_bias, _mode="full"):
    inp = dict(L0_attn_x=L0_attn_x, L0_attn_wq=L0_attn_wq, L0_attn_wk=L0_attn_wk,
               L0_attn_wv=L0_attn_wv, L0_attn_wo=L0_attn_wo,
               L0_ffn_shared_w1=L0_ffn_shared_w1, L0_ffn_shared_b1=L0_ffn_shared_b1,
               L0_ffn_shared_w2=L0_ffn_shared_w2, L0_ffn_shared_b2=L0_ffn_shared_b2,
               L0_ffn_routed_w1=L0_ffn_routed_w1, L0_ffn_routed_b1=L0_ffn_routed_b1,
               L0_ffn_routed_w2=L0_ffn_routed_w2, L0_ffn_routed_b2=L0_ffn_routed_b2,
               L0_ffn_centroids=L0_ffn_centroids, L0_ffn_bias=L0_ffn_bias)
    shared = _prep_shared(inp)

    x = np.asarray(L0_attn_x, dtype=np.float32).reshape(4 * 4096, D)
    in_maps = []
    for c in range(NCORE):
        xs = _arr(x[c * NTOK:(c + 1) * NTOK, :].T)
        cos, sin = _rope_tables((c % 2) * NTOK)
        even = 1.0 if c % 2 == 0 else 0.0
        m = dict(shared)
        m["xT"] = xs
        m["costab"] = cos
        m["sintab"] = sin
        m["msend"] = np.full((128, 1), even, np.float32)
        m["mrecv"] = np.full((128, 1), 1.0 - even, np.float32)
        in_maps.append(m)

    nc = build_program(mode=_mode)
    res = run_bass_kernel_spmd(nc, in_maps, core_ids=list(range(NCORE)))
    out = np.empty((4 * 4096, D), dtype=np.float32)
    for c in range(NCORE):
        out[c * NTOK:(c + 1) * NTOK, :] = res.results[c]["y"].T
    return out.reshape(4, 4096, D)


# version 03
# speedup vs baseline: 1.0457x; 1.0457x over previous
import contextlib
import math
import os

ATTN_DBG = int(os.environ.get("ATTN_DBG", "4"))
ATTN_SUB = os.environ.get("ATTN_SUB", "kvqor")
ATTN_ROPE = int(os.environ.get("ATTN_ROPE", "5"))
ATTN_QH = int(os.environ.get("ATTN_QH", "4"))

import numpy as np

import concourse.bass as bass
import concourse.mybir as mybir
from concourse.bass_utils import run_bass_kernel_spmd

F32 = mybir.dt.float32
BF16 = mybir.dt.bfloat16
AF = mybir.ActivationFunctionType
ALU = mybir.AluOpType
AX = mybir.AxisListType

D = 2048
DCH = 16
FF = 1408
FCH = 11
NTOK = 2048
FT = 1024
AT = 512
NR = 16
NS = 2
KR = 4
H = 16
HKV = 4
DH = 128
CH = 64
NCORE = 8

POOL_LANES = tuple(f"pool{i}" for i in range(8))
SYNC_LANES = tuple(f"sync{i}" for i in range(4))
ENGINES = ("pe", "act", "dve") + POOL_LANES + SYNC_LANES + ("cc",)
PHYS = {}
for _e in ("pe", "act", "dve"):
    PHYS[_e] = _e
for _e in POOL_LANES:
    PHYS[_e] = "pool"
for _e in SYNC_LANES:
    PHYS[_e] = "sync"
PHYS["cc"] = "pool"


class Buf:
    __slots__ = ("writer", "readers", "init", "psum")

    def __init__(self, init=(), psum=False):
        self.writer = None
        self.readers = []
        self.init = list(init)
        self.psum = psum


def rebind(P, bufs):
    init = []
    for b in bufs:
        if b.writer is not None:
            init.append(b.writer)
        init.extend(b.readers)
        init.extend(b.init)
    return Buf(init=P.reduce_gids(init))


class Op:
    __slots__ = ("gid", "eng", "seq", "fn", "inc", "deps", "after", "marked",
                 "wait_list", "semval")

    def __init__(self, gid, eng, seq, fn, inc, deps, after):
        self.gid = gid
        self.eng = eng
        self.seq = seq
        self.fn = fn
        self.inc = inc
        self.deps = deps
        self.after = after
        self.marked = False
        self.wait_list = []
        self.semval = 0


class Prog:
    def __init__(self):
        self.ops = []
        self.eng_ops = {e: [] for e in ENGINES}
        self.clock = {e: {e2: -1 for e2 in ENGINES} for e in ENGINES}
        self.rr = {"pool": 0, "sync": 0}
        self.last_on_lane = {}

    def reduce_gids(self, gids):
        best = {}
        for g in gids:
            o = self.ops[g]
            if o.eng not in best or self.ops[best[o.eng]].seq < o.seq:
                best[o.eng] = g
        return list(best.values())

    def op(self, eng, fn, reads=(), writes=(), inc=1):
        extra = ()
        if eng in ("pool", "sync"):
            if inc == 16:
                lanes = POOL_LANES if eng == "pool" else SYNC_LANES
                lane = lanes[self.rr[eng] % len(lanes)]
                self.rr[eng] += 1
                if lane in self.last_on_lane:
                    extra = (self.last_on_lane[lane],)
                self.last_on_lane[lane] = len(self.ops)
                eng = lane
            else:
                eng = (POOL_LANES if eng == "pool" else SYNC_LANES)[0]
        gid = len(self.ops)
        seq = len(self.eng_ops[eng])
        dep_gids = set(extra)
        for b in reads:
            if b.writer is not None:
                dep_gids.add(b.writer)
            dep_gids.update(b.init)
            if b.psum and eng in ("act", "dve"):
                for g in b.readers:
                    if self.ops[g].eng in ("act", "dve") and self.ops[g].eng != eng:
                        dep_gids.add(g)
        for b in writes:
            if b.writer is not None:
                dep_gids.add(b.writer)
            dep_gids.update(b.readers)
            dep_gids.update(b.init)
        cl = self.clock[eng]
        deps = []
        for g in sorted(self.reduce_gids(dep_gids)):
            o = self.ops[g]
            if o.eng == "pe" and eng == "pe":
                continue
            if cl[o.eng] >= o.seq:
                continue
            deps.append(g)
            for e2 in ENGINES:
                if o.after[e2] > cl[e2]:
                    cl[e2] = o.after[e2]
        after = dict(cl)
        after[eng] = seq
        op = Op(gid, eng, seq, fn, inc, deps, after)
        self.ops.append(op)
        self.eng_ops[eng].append(op)
        for b in reads:
            b.readers.append(gid)
        for b in writes:
            b.writer = gid
            b.readers = []
        return gid

    def finalize(self):
        for op in self.ops:
            if op.inc == 16:
                op.marked = True
            for g in op.deps:
                self.ops[g].marked = True
        cum = {e: 0 for e in ENGINES}
        for op in self.ops:
            if op.marked:
                cum[op.eng] += op.inc
                op.semval = cum[op.eng]
        for op in self.ops:
            op.wait_list = [(self.ops[g].eng, self.ops[g].semval)
                            for g in op.deps]
        self.n_marked = sum(1 for o in self.ops if o.marked)

    def replay(self, nc, sems):
        self.finalize()
        with nc.Block() as block:
            def mk(phys):
                ops = sorted((o for o in self.ops if PHYS[o.eng] == phys),
                             key=lambda o: o.gid)

                def body(eng):
                    for op in ops:
                        for (pe_eng, val) in op.wait_list:
                            eng.wait_ge(sems[pe_eng], val)
                        ins = op.fn(eng)
                        if op.marked and ins is not None:
                            ins.then_inc(sems[op.eng], op.inc)
                return body
            block.tensor(mk("pe"))
            block.scalar(mk("act"))
            block.vector(mk("dve"))
            block.gpsimd(mk("pool"))
            block.sync(mk("sync"))



def build_program(mode="full"):
    nc = bass.Bass("TRN2", target_bir_lowering=False, debug=False)

    xin = nc.dram_tensor("xT", [D, NTOK], F32, kind="ExternalInput").ap()
    yout = nc.dram_tensor("y", [D, NTOK], F32, kind="ExternalOutput").ap()
    x1 = nc.dram_tensor("x1d", [D, NTOK], F32).ap()
    x2 = nc.dram_tensor("x2d", [D, NTOK], F32).ap()
    cc_in = nc.dram_tensor("cc_in", [128, H * DH], F32).ap()
    cc_out = nc.dram_tensor("cc_out", [128, H * DH], F32).ap()

    rw1 = nc.dram_tensor("rw1", [NR, FCH, DCH, 128, 128], F32, kind="ExternalInput").ap()
    sw1 = nc.dram_tensor("sw1", [NS, FCH, DCH, 128, 128], F32, kind="ExternalInput").ap()
    rw2 = nc.dram_tensor("rw2", [NR, DCH, FCH, 128, 128], F32, kind="ExternalInput").ap()
    sw2 = nc.dram_tensor("sw2", [NS, DCH, FCH, 128, 128], F32, kind="ExternalInput").ap()
    b1in = nc.dram_tensor("b1a", [128, (NR + NS) * FCH], F32, kind="ExternalInput").ap()
    rb2 = nc.dram_tensor("rb2", [NR, D], F32, kind="ExternalInput").ap()
    sb2 = nc.dram_tensor("sb2", [NS, D], F32, kind="ExternalInput").ap()
    centin = nc.dram_tensor("centT", [DCH, 128, NR], F32, kind="ExternalInput").ap()
    biasin = nc.dram_tensor("rbias", [NR, 1], F32, kind="ExternalInput").ap()
    wq_in = nc.dram_tensor("wqT", [H, DCH, 128, 128], F32, kind="ExternalInput").ap()
    wk_in = nc.dram_tensor("wkT", [HKV, DCH, 128, 128], F32, kind="ExternalInput").ap()
    wv_in = nc.dram_tensor("wvT", [HKV, DCH, 128, 128], F32, kind="ExternalInput").ap()
    wo_in = nc.dram_tensor("woT", [DCH, H, 128, 128], F32, kind="ExternalInput").ap()
    cosin = nc.dram_tensor("costab", [128, NTOK], F32, kind="ExternalInput").ap()
    sinin = nc.dram_tensor("sintab", [128, NTOK], F32, kind="ExternalInput").ap()
    diagin = nc.dram_tensor("diagT", [CH, H * CH], F32, kind="ExternalInput").ap()
    qdecin = nc.dram_tensor("qdec", [128, H * CH], F32, kind="ExternalInput").ap()
    kdecin = nc.dram_tensor("kdec", [CH, H], F32, kind="ExternalInput").ap()
    eyein = nc.dram_tensor("eye", [128, 128], F32, kind="ExternalInput").ap()
    rotin = nc.dram_tensor("rotR", [128, 128], F32, kind="ExternalInput").ap()
    eselin = nc.dram_tensor("esel", [NR, NR * 128], F32, kind="ExternalInput").ap()
    onescin = nc.dram_tensor("onesc", [128, 1], F32, kind="ExternalInput").ap()
    onesrin = nc.dram_tensor("onesr", [1, 128], F32, kind="ExternalInput").ap()
    msendin = nc.dram_tensor("msend", [128, 1], F32, kind="ExternalInput").ap()
    btilein = nc.dram_tensor("btile_in", [128, NR], F32, kind="ExternalInput").ap()
    mrecvin = nc.dram_tensor("mrecv", [128, 1], F32, kind="ExternalInput").ap()

    lam = [2.0 ** (-8.0 * (h + 1) / 16.0) for h in range(H)]
    blk = [math.exp(-CH * lam[h]) for h in range(H)]

    es = contextlib.ExitStack()
    ec = es.enter_context
    with es:
        xsb = ec(nc.sbuf_tensor("xsb", [128, DCH * FT], F32))
        xn = ec(nc.sbuf_tensor("xn", [128, DCH * FT], BF16))
        hq = ec(nc.sbuf_tensor("hq", [128, FCH * FT], BF16))
        kfm = ec(nc.sbuf_tensor("kfm", [128, HKV * AT], BF16))
        vfm = ec(nc.sbuf_tensor("vfm", [128, HKV * AT], BF16))
        ktok = [ec(nc.sbuf_tensor(f"ktok{i}", [CH, HKV * DH], BF16)) for i in range(2)]
        vtok = [ec(nc.sbuf_tensor(f"vtok{i}", [CH, HKV * DH], BF16)) for i in range(2)]
        kvsb = ec(nc.sbuf_tensor("kvsb", [128, H * DH], F32))
        kvbf = [ec(nc.sbuf_tensor(f"kvbf{i}", [128, DH], BF16)) for i in range(2)]
        qtil = [ec(nc.sbuf_tensor(f"qtil{i}", [128, CH], BF16)) for i in range(2)]
        ktil = [ec(nc.sbuf_tensor(f"ktil{i}", [CH, DH], BF16)) for i in range(2)]
        scbf = [ec(nc.sbuf_tensor(f"scbf{i}", [CH, CH], BF16)) for i in range(2)]
        wb = [ec(nc.sbuf_tensor(f"wb{i}", [128, 2048], BF16)) for i in range(3)]
        gbf = ec(nc.sbuf_tensor("gbf16", [128, FT], BF16))
        scr = [ec(nc.sbuf_tensor(f"scr{i}", [128, 512], F32)) for i in range(4)]
        rows = ec(nc.sbuf_tensor("rows", [1, 4 * 512], F32))
        gT = ec(nc.sbuf_tensor("gT", [NR, FT], BF16))
        gtok_o = ec(nc.sbuf_tensor("gtok_o", [128, NR], F32))
        gtok_w = ec(nc.sbuf_tensor("gtok_w", [128, NR], F32))
        gtmp = ec(nc.sbuf_tensor("gtmp", [128, NR], F32))
        gm = ec(nc.sbuf_tensor("gm", [128, 8], F32))
        gbf16t = ec(nc.sbuf_tensor("gbf16t", [128, NR], BF16))
        onesc = ec(nc.sbuf_tensor("onesc_s", [128, 1], F32))
        onesr = ec(nc.sbuf_tensor("onesr_s", [1, 128], F32))
        eye_b = ec(nc.sbuf_tensor("eye_b", [128, 128], BF16))
        eye_f = ec(nc.sbuf_tensor("eye_f", [128, 128], F32))
        rot_b = ec(nc.sbuf_tensor("rot_b", [128, 128], BF16))
        b1sb = ec(nc.sbuf_tensor("b1sb", [128, (NR + NS) * FCH], F32))
        b2r_s = ec(nc.sbuf_tensor("b2r_s", [NR, D], BF16))
        b2s_s = ec(nc.sbuf_tensor("b2s_s", [NS, D], BF16))
        ones2 = ec(nc.sbuf_tensor("ones2", [NS, 512], BF16))
        esel = ec(nc.sbuf_tensor("esel_s", [NR, NR * 128], BF16))
        cent = ec(nc.sbuf_tensor("cent_s", [128, DCH * NR], BF16))
        biasr = ec(nc.sbuf_tensor("biasr", [NR, 1], F32))
        kdec = ec(nc.sbuf_tensor("kdec_s", [CH, H], F32))
        diag = ec(nc.sbuf_tensor("diag_s", [CH, H * CH], BF16))
        qdec = ec(nc.sbuf_tensor("qdec_s", [128, H * CH], BF16))
        msend = ec(nc.sbuf_tensor("msend_s", [128, 1], F32))
        mrecv = ec(nc.sbuf_tensor("mrecv_s", [128, 1], F32))
        costl = ec(nc.sbuf_tensor("costl", [128, 512], F32))
        epsc = ec(nc.sbuf_tensor("epsc", [1, 1], F32))
        sbf = [ec(nc.sbuf_tensor(f"sbf{i}", [128, 512], BF16)) for i in range(2)]
        onescb = ec(nc.sbuf_tensor("onescb", [128, 1], BF16))
        gall = ec(nc.sbuf_tensor("gall", [128, 8 * NR], BF16))
        btile = ec(nc.sbuf_tensor("btile", [128, NR], F32))
        gtok2_o = ec(nc.sbuf_tensor("gtok2_o", [128, NR], F32))
        gtok2_w = ec(nc.sbuf_tensor("gtok2_w", [128, NR], F32))
        gtmp2 = ec(nc.sbuf_tensor("gtmp2", [128, NR], F32))
        gm2 = ec(nc.sbuf_tensor("gm2", [128, 8], F32))
        sintl = ec(nc.sbuf_tensor("sintl", [128, 512], F32))
        pp = [ec(nc.psum_tensor(f"pp{i}", [128, 512], F32)) for i in range(6)]
        pb = [ec(nc.psum_tensor(f"pbf{i}", [128, 1024], BF16)) for i in range(2)]

        sems = {e: ec(nc.semaphore(f"sem_{e}")) for e in ENGINES}

        P = Prog()

        bx = [Buf() for _ in range(DCH)]
        bxn = [Buf() for _ in range(DCH)]
        bh = [Buf() for _ in range(FCH)]
        bkfm = [Buf() for _ in range(HKV)]
        bvfm = [Buf() for _ in range(HKV)]
        bktok = [Buf(), Buf()]
        bvtok = [Buf(), Buf()]
        bkv = [Buf() for _ in range(H)]
        bkvbf = [Buf(), Buf()]
        bqtil = [Buf(), Buf()]
        bktil = [Buf(), Buf()]
        bscbf = [Buf(), Buf()]
        bwb = [Buf() for _ in range(3)]
        bgbf = Buf()
        bscr = [Buf() for _ in range(4)]
        brows = [Buf() for _ in range(4)]
        bgT = [Buf(), Buf()]
        bgtok_o = [Buf(), Buf()]
        bgtok_w = [Buf(), Buf()]
        bgtmp = [Buf(), Buf()]
        bgm = [[Buf() for _ in range(8)] for _ in range(2)]
        bgall = [Buf() for _ in range(8)]
        bsbf = [Buf(), Buf()]
        bconst = Buf()
        bpp = [Buf(psum=True) for _ in range(6)]
        bpb = [Buf(psum=True), Buf(psum=True)]
        bcos = Buf(); bsin = Buf()
        bx1 = [Buf() for _ in range(4)]
        bx2 = [Buf() for _ in range(4)]
        bccin = Buf(); bccout = Buf()
        bout = [Buf() for _ in range(4)]

        wb_ctr = [0]

        def next_wb():
            i = wb_ctr[0] % 3
            wb_ctr[0] += 1
            return i

        def ld(engine, dst_ap, src_ap, buf):
            P.op(engine, lambda eng, d=dst_ap, s=src_ap: eng.dma_start(d, s),
                 reads=(), writes=(buf,), inc=16)

        ld("sync", onesc[:], onescin[:], bconst)
        ld("sync", onesr[:], onesrin[:], bconst)
        ld("sync", eye_f[:], eyein[:], bconst)
        ld("sync", b1sb[:], b1in[:], bconst)
        ld("sync", biasr[:], biasin[:], bconst)
        ld("sync", kdec[:], kdecin[:], bconst)
        ld("sync", msend[:], msendin[:], bconst)
        ld("sync", mrecv[:], mrecvin[:], bconst)
        ld("pool", eye_b[:], eyein[:], bconst)
        ld("pool", rot_b[:], rotin[:], bconst)
        ld("pool", b2r_s[:], rb2[:], bconst)
        ld("pool", b2s_s[:], sb2[:], bconst)
        ld("pool", esel[:], eselin[:], bconst)
        ld("pool", cent[:].rearrange("p (c e) -> p c e", e=NR),
           centin.rearrange("c p e -> p c e"), bconst)
        ld("pool", diag[:], diagin[:], bconst)
        ld("pool", qdec[:], qdecin[:], bconst)
        P.op("dve", lambda eng: eng.memset(ones2[:], 1.0), writes=(bconst,))
        ld("pool", onescb[:], onescin[:], bconst)
        ld("sync", btile[:], btilein[:], bconst)
        P.op("dve", lambda eng: eng.memset(epsc[:], 1e-5), writes=(bconst,))

        def emit_ln(xsrc3, xdst_bufs_dram, tcols, width, update_x):
            nsub = width // 512
            for sc in range(nsub):
                c0 = tcols + sc * 512
                ps1 = pp[4]; ps2 = pp[5]
                for ch in range(DCH):
                    xap = xsb[:, ch * FT + c0: ch * FT + c0 + 512]
                    xb = sbf[ch % 2]
                    P.op("act", lambda eng, o=xb[:], i=xap: eng.copy(o, i),
                         reads=(bx[ch],), writes=(bsbf[ch % 2],))
                    P.op("pe", lambda eng, o=ps1[0:1, :], b=xb[:],
                         s=(ch == 0), t=(ch == DCH - 1):
                         eng.matmul(o, onescb[:], b, start=s, stop=t),
                         reads=(bconst, bsbf[ch % 2]), writes=(bpp[4],))
                    sq = scr[ch % 2]
                    P.op("act", lambda eng, o=sq[0:128, 0:512], i=xap:
                         eng.activation(o, i, AF.Square),
                         reads=(bx[ch],), writes=(bscr[ch % 2],))
                    P.op("pe", lambda eng, o=ps2[0:1, :], a=onesc[:], b=sq[:],
                         s=(ch == 0), t=(ch == DCH - 1):
                         eng.matmul(o, a, b, start=s, stop=t),
                         reads=(bconst, bscr[ch % 2]), writes=(bpp[5],))
                r0 = rows[:, 0:512]; r1 = rows[:, 512:1024]
                r2 = rows[:, 1024:1536]; r3 = rows[:, 1536:2048]
                P.op("dve", lambda eng: eng.tensor_copy(r0, ps1[0:1, :]),
                     reads=(bpp[4],), writes=(brows[0],))
                P.op("dve", lambda eng: eng.tensor_copy(r1, ps2[0:1, :]),
                     reads=(bpp[5],), writes=(brows[1],))
                inv = 1.0 / D
                P.op("dve", lambda eng: eng.tensor_scalar_mul(r0, r0, inv),
                     reads=(brows[0],), writes=(brows[0],))
                P.op("dve", lambda eng: eng.tensor_scalar_mul(r1, r1, inv),
                     reads=(brows[1],), writes=(brows[1],))
                P.op("dve", lambda eng: eng.tensor_mul(r2, r0, r0),
                     reads=(brows[0],), writes=(brows[2],))
                P.op("dve", lambda eng: eng.tensor_sub(r2, r1, r2),
                     reads=(brows[1], brows[2]), writes=(brows[2],))
                P.op("act", lambda eng: eng.activation(r2, r2, AF.Sqrt,
                                                       bias=epsc[0:1, 0:1]),
                     reads=(brows[2], bconst), writes=(brows[2],))
                P.op("dve", lambda eng: eng.reciprocal(r2, r2),
                     reads=(brows[2],), writes=(brows[2],))
                P.op("dve", lambda eng: eng.scalar_tensor_tensor(
                    r3, r2, -1.0, r0, ALU.mult, ALU.mult),
                    reads=(brows[2], brows[0]), writes=(brows[3],))
                psA = pp[4]; psB = pp[5]
                P.op("pe", lambda eng: eng.matmul(psA[:, :], onesr[:], r2,
                                                  start=True, stop=True),
                     reads=(bconst, brows[2]), writes=(bpp[4],))
                P.op("pe", lambda eng: eng.matmul(psB[:, :], onesr[:], r3,
                                                  start=True, stop=True),
                     reads=(bconst, brows[3]), writes=(bpp[5],))
                for ch in range(DCH):
                    xap = xsb[:, ch * FT + c0: ch * FT + c0 + 512]
                    xnap = xn[:, ch * FT + c0: ch * FT + c0 + 512]
                    t = scr[2 + ch % 2]
                    P.op("dve", lambda eng, t=t[:], x=xap: eng.tensor_mul(
                        t, x, psA[:, :]),
                        reads=(bx[ch], bpp[4]), writes=(bscr[2 + ch % 2],))
                    P.op("dve", lambda eng, t=t[:], o=xnap: eng.tensor_add(
                        o, t, psB[:, :]),
                        reads=(bscr[2 + ch % 2], bpp[5]), writes=(bxn[ch],))
                    if update_x:
                        P.op("dve", lambda eng, x=xap, o=xnap:
                             eng.scalar_tensor_tensor(x, o, 0.5, x,
                                                      ALU.mult, ALU.add),
                             reads=(bxn[ch], bx[ch]), writes=(bx[ch],))

        def emit_gating_scores(tcols):
            for mc in range(8):
                col = tcols + mc * 128
                par = mc % 2
                pt = pp[4 + par]
                for ch in range(DCH):
                    P.op("pe", lambda eng, a=xn[:, ch * FT + col: ch * FT + col + 128],
                         r=cent[:, ch * NR:(ch + 1) * NR], o=pt[:, 0:NR],
                         st=(ch == 0), sp=(ch == DCH - 1):
                         eng.matmul(o, a, r, start=st, stop=sp),
                         reads=(bconst, bxn[ch]), writes=(bpp[4 + par],))
                go = (gtok_o, gtok2_o)[par]; gw = (gtok_w, gtok2_w)[par]
                gt = (gtmp, gtmp2)[par]; gmm = (gm, gm2)[par]
                bgo = bgtok_o[par]; bgw = bgtok_w[par]
                bgt = bgtmp[par]; bgmm = bgm[par]
                P.op("act", lambda eng, o=go[:], i=pt[:, 0:NR]:
                     eng.activation(o, i, AF.Sigmoid),
                     reads=(bpp[4 + par],), writes=(bgo,))
                P.op("dve", lambda eng, o=go[:]: eng.tensor_add(o, o, btile[:]),
                     reads=(bgo, bconst), writes=(bgo,))
                P.op("dve", lambda eng, o=gw[:], i=go[:]: eng.tensor_copy(o, i),
                     reads=(bgo,), writes=(bgw,))
                for r in range(KR):
                    P.op("dve", lambda eng, m=gmm[:, r:r + 1], i=gw[:]:
                         eng.reduce_max(m, i, axis=AX.X),
                         reads=(bgw,), writes=(bgmm[r],))
                    if r < KR - 1:
                        P.op("dve", lambda eng, m=gmm[:, r:r + 1], o=gt[:], i=gw[:]:
                             eng.tensor_scalar(o, i, m, None, ALU.is_ge),
                             reads=(bgw, bgmm[r]), writes=(bgt,))
                        P.op("dve", lambda eng, o=gw[:], t=gt[:]:
                             eng.scalar_tensor_tensor(o, t, -1e30, o,
                                                      ALU.mult, ALU.add),
                             reads=(bgt, bgw), writes=(bgw,))
                P.op("dve", lambda eng, o=gt[:], i=go[:], m=gmm[:, 3:4]:
                     eng.tensor_scalar(o, i, m, None, ALU.is_ge),
                     reads=(bgo, bgmm[3]), writes=(bgt,))
                P.op("dve", lambda eng, o=go[:], t=gt[:]:
                     eng.tensor_mul(o, o, t),
                     reads=(bgo, bgt), writes=(bgo,))
                P.op("dve", lambda eng, m=gmm[:, 4:5], i=go[:]:
                     eng.reduce_sum(m, i, axis=AX.X),
                     reads=(bgo,), writes=(bgmm[4],))
                P.op("dve", lambda eng, m=gmm[:, 4:5]:
                     eng.tensor_scalar_max(m, m, 1e-8),
                     reads=(bgmm[4],), writes=(bgmm[4],))
                P.op("dve", lambda eng, o=gmm[:, 5:6], i=gmm[:, 4:5]:
                     eng.reciprocal(o, i),
                     reads=(bgmm[4],), writes=(bgmm[5],))
                P.op("dve", lambda eng, o=gall[:, mc * NR:(mc + 1) * NR],
                     i=go[:], m=gmm[:, 5:6]:
                     eng.tensor_scalar(o, i, m, None, ALU.mult),
                     reads=(bgo, bgmm[5]), writes=(bgall[mc],))

        def emit_gating_transposes(tcols):
            for mc in range(8):
                sc = (mc * 128) // 512
                m0 = (mc * 128) % 512
                P.op("pe", lambda eng, i=gall[:, mc * NR:(mc + 1) * NR]:
                     eng.transpose(pb[0][0:NR, 0:128], i, eye_b[:]),
                     reads=(bgall[mc], bconst), writes=(bpb[0],))
                P.op("act", lambda eng, o=gT[:, sc * 512 + m0: sc * 512 + m0 + 128]:
                     eng.copy(o, pb[0][0:NR, 0:128]),
                     reads=(bpb[0],), writes=(bgT[sc],))

        def emit_expert(kind, j, tcols):
            w1ap = (sw1 if kind == "s" else rw1)
            w2ap = (sw2 if kind == "s" else rw2)
            bcol0 = (NR * FCH + j * FCH) if kind == "s" else j * FCH
            if kind == "r":
                for sc in range(2):
                    psg = pp[2 + sc]
                    P.op("pe", lambda eng, a=esel[:, j * 128:(j + 1) * 128],
                         g=gT[:, sc * 512:(sc + 1) * 512], o=psg[:, :]:
                         eng.matmul(o, a, g, start=True, stop=True),
                         reads=(bconst, bgT[sc]), writes=(bpp[2 + sc],))
                    P.op("act", lambda eng, o=gbf[:, sc * 512:(sc + 1) * 512],
                         i=psg[:, :]: eng.copy(o, i),
                         reads=(bpp[2 + sc],), writes=(bgbf,))
            for fb in range(FCH):
                wi = next_wb()
                src = w1ap[j, fb].rearrange("c p f -> p c f")
                P.op("pool", lambda eng,
                     d=wb[wi][:, 0:2048].rearrange("p (c f) -> p c f", f=128), s=src:
                     eng.dma_start(d, s), reads=(), writes=(bwb[wi],), inc=16)
                for sc in range(2):
                    psh = pp[sc]
                    c0 = tcols + sc * 512
                    for ch in range(DCH):
                        P.op("pe", lambda eng, a=wb[wi][:, ch * 128:(ch + 1) * 128],
                             b=xn[:, ch * FT + c0: ch * FT + c0 + 512],
                             o=psh[:, :], s=(ch == 0), t=(ch == DCH - 1):
                             eng.matmul(o, a, b, start=s, stop=t),
                             reads=(bwb[wi], bxn[ch]), writes=(bpp[sc],))
                    hap = hq[:, fb * FT + sc * 512: fb * FT + sc * 512 + 512]
                    P.op("act", lambda eng, o=hap, i=psh[:, :],
                         b=b1sb[:, bcol0 + fb: bcol0 + fb + 1]:
                         eng.activation(o, i, AF.Relu, bias=b),
                         reads=(bpp[sc], bconst), writes=(bh[fb],))
                    if kind == "r":
                        P.op("dve", lambda eng, o=hap,
                             g=gbf[:, sc * 512:(sc + 1) * 512]:
                             eng.tensor_mul(o, o, g),
                             reads=(bh[fb], bgbf), writes=(bh[fb],))
            for db in range(DCH):
                wi = next_wb()
                src = w2ap[j, db].rearrange("c p f -> p c f")
                P.op("pool", lambda eng,
                     d=wb[wi][:, 0:FCH * 128].rearrange("p (c f) -> p c f", f=128), s=src:
                     eng.dma_start(d, s), reads=(), writes=(bwb[wi],), inc=16)
                for sc in range(2):
                    psy = pp[2 + (db % 2) * 2 + sc]
                    bpsy = bpp[2 + (db % 2) * 2 + sc]
                    c0 = tcols + sc * 512
                    first = True
                    if kind == "s" and j == 0:
                        P.op("pe", lambda eng, a=b2s_s[:, db * 128:(db + 1) * 128],
                             o=psy[:, :]: eng.matmul(o, a, ones2[:, :],
                                                     start=True, stop=False),
                             reads=(bconst,), writes=(bpsy,))
                        first = False
                    if kind == "r" and j == 0:
                        P.op("pe", lambda eng, a=b2r_s[:, db * 128:(db + 1) * 128],
                             g=gT[:, sc * 512:(sc + 1) * 512], o=psy[:, :]:
                             eng.matmul(o, a, g, start=True, stop=False),
                             reads=(bconst, bgT[sc]), writes=(bpsy,))
                        first = False
                    for fb in range(FCH):
                        hap = hq[:, fb * FT + sc * 512: fb * FT + sc * 512 + 512]
                        P.op("pe", lambda eng, a=wb[wi][:, fb * 128:(fb + 1) * 128],
                             b=hap, o=psy[:, :], s=(first and fb == 0),
                             t=(fb == FCH - 1):
                             eng.matmul(o, a, b, start=s, stop=t),
                             reads=(bwb[wi], bh[fb]), writes=(bpsy,))
                    xap = xsb[:, db * FT + c0: db * FT + c0 + 512]
                    P.op("dve", lambda eng, x=xap, p=psy[:, :]:
                         eng.scalar_tensor_tensor(x, p, 0.5, x, ALU.mult, ALU.add),
                         reads=(bpsy, bx[db]), writes=(bx[db],))

        def load_x_tile(src, tcols, width):
            src3 = src.rearrange("(c p) t -> p c t", p=128)
            dst3 = xsb[:, :].rearrange("p (c t) -> p c t", t=FT)
            return (dst3[:, :, 0:width], src3[:, :, tcols:tcols + width])

        def store_x_tile(dst, tcols, width):
            src3 = xsb[:, :].rearrange("p (c t) -> p c t", t=FT)
            dst3 = dst.rearrange("(c p) t -> p c t", p=128)
            return (dst3[:, :, tcols:tcols + width], src3[:, :, 0:width])

        def emit_ffn(src, dst, src_bufs, dst_bufs):
            for tile in range(2):
                tc = tile * FT
                d, s = load_x_tile(src, tc, FT)
                P.op("sync", lambda eng, d=d, s=s: eng.dma_start(d, s),
                     reads=tuple(src_bufs[2 * tile: 2 * tile + 2]),
                     writes=tuple(bx), inc=16)
                emit_ln(None, None, 0, FT, update_x=True)
                emit_gating_scores(0)
                for (kind, j) in [("s", 0), ("s", 1)]:
                    emit_expert(kind, j, 0)
                emit_gating_transposes(0)
                for (kind, j) in [("r", jj) for jj in range(NR)]:
                    emit_expert(kind, j, 0)
                d, s = store_x_tile(dst, tc, FT)
                P.op("sync", lambda eng, d=d, s=s: eng.dma_start(d, s),
                     reads=tuple(bx),
                     writes=tuple(dst_bufs[2 * tile: 2 * tile + 2]), inc=16)

        def proj_rope(w_ap, nheads, dst_sb, dst_bufs, do_rope, at):
            for hh in range(nheads):
                wi = next_wb()
                src = w_ap[hh].rearrange("c p f -> p c f")
                P.op("pool", lambda eng,
                     d=wb[wi][:, 0:2048].rearrange("p (c f) -> p c f", f=128), s=src:
                     eng.dma_start(d, s), reads=(), writes=(bwb[wi],), inc=16)
                psp = pp[hh % 2]
                for ch in range(DCH):
                    P.op("pe", lambda eng, a=wb[wi][:, ch * 128:(ch + 1) * 128],
                         b=xn[:, ch * FT: ch * FT + 512], o=psp[:, :],
                         s=(ch == 0), t=(ch == DCH - 1):
                         eng.matmul(o, a, b, start=s, stop=t),
                         reads=(bwb[wi], bxn[ch]), writes=(bpp[hh % 2],))
                if do_rope:
                    kraw = hq[:, 8 * FT: 8 * FT + 512]
                    t0 = scr[hh % 2]; t1 = scr[2 + hh % 2]
                    P.op("dve", lambda eng, t=t0[:], p=psp[:, :]:
                         eng.tensor_mul(t, p, costl[:]),
                         reads=(bpp[hh % 2], bcos), writes=(bscr[hh % 2],))
                    if ATTN_ROPE >= 2:
                        P.op("act", lambda eng, i=psp[:, :]: eng.copy(kraw, i),
                             reads=(bpp[hh % 2],), writes=(bh[8],))
                    if ATTN_ROPE >= 3:
                        psr = pp[2 + hh % 2]
                        P.op("pe", lambda eng, o=psr[:, :]:
                             eng.matmul(o, rot_b[:], kraw, start=True, stop=True),
                             reads=(bconst, bh[8]), writes=(bpp[2 + hh % 2],))
                    if ATTN_ROPE >= 4:
                        P.op("dve", lambda eng, t=t1[:], p=psr[:, :]:
                             eng.tensor_mul(t, p, sintl[:]),
                             reads=(bpp[2 + hh % 2], bsin), writes=(bscr[2 + hh % 2],))
                    if ATTN_ROPE >= 5:
                        P.op("dve", lambda eng, o=dst_sb[:, hh * 512:(hh + 1) * 512],
                             a=t0[:], b=t1[:]: eng.tensor_add(o, a, b),
                             reads=(bscr[hh % 2], bscr[2 + hh % 2]),
                             writes=tuple(dst_bufs(hh)))
                    else:
                        P.op("dve", lambda eng, o=dst_sb[:, hh * 512:(hh + 1) * 512],
                             a=t0[:]: eng.tensor_copy(o, a),
                             reads=(bscr[hh % 2],),
                             writes=tuple(dst_bufs(hh)))
                else:
                    P.op("act", lambda eng, o=dst_sb[:, hh * 512:(hh + 1) * 512],
                         i=psp[:, :]: eng.copy(o, i),
                         reads=(bpp[hh % 2],), writes=tuple(dst_bufs(hh)))

        def emit_attn_phase(phase, at, bqo=None):
            tc = at * AT
            d, s = load_x_tile(x1 if not mode.startswith('attn') else xin, tc, AT)
            P.op("sync", lambda eng, d=d, s=s: eng.dma_start(d, s),
                 reads=(bx1[at],) if not mode.startswith('attn') else (),
                 writes=tuple(bx), inc=16)
            emit_ln(None, None, 0, AT, update_x=False)
            P.op("sync", lambda eng: eng.dma_start(costl[:], cosin[:, tc:tc + AT]),
                 reads=(bcos,), writes=(bcos,), inc=16)
            P.op("sync", lambda eng: eng.dma_start(sintl[:], sinin[:, tc:tc + AT]),
                 reads=(bsin,), writes=(bsin,), inc=16)
            if ATTN_DBG >= 2:
                rope_on = "r" in ATTN_SUB
                if "k" in ATTN_SUB:
                    proj_rope(wk_in, HKV, kfm, lambda h: (bkfm[h],), rope_on, at)
                if "v" in ATTN_SUB:
                    proj_rope(wv_in, HKV, vfm, lambda h: (bvfm[h],), False, at)
                if phase == "B" and "q" in ATTN_SUB:
                    proj_rope(wq_in, H, hq,
                              lambda h: tuple(bqo[h * 8 + c] for c in range(8)), rope_on, at)
            for c in range(8 if ATTN_DBG >= 3 else 0):
                pg = c % 2
                for kvh in range(HKV):
                    P.op("pe", lambda eng, i=kfm[:, kvh * AT + c * CH: kvh * AT + c * CH + CH],
                         o=pb[0][0:CH, kvh * 128:(kvh + 1) * 128]:
                         eng.transpose(o, i, eye_b[:]),
                         reads=(bkfm[kvh], bconst), writes=(bpb[0],))
                    P.op("pe", lambda eng, i=vfm[:, kvh * AT + c * CH: kvh * AT + c * CH + CH],
                         o=pb[1][0:CH, kvh * 128:(kvh + 1) * 128]:
                         eng.transpose(o, i, eye_b[:]),
                         reads=(bvfm[kvh], bconst), writes=(bpb[1],))
                P.op("act", lambda eng, o=ktok[pg][:, :], i=pb[0][0:CH, 0:HKV * 128]:
                     eng.copy(o, i), reads=(bpb[0],), writes=(bktok[pg],))
                P.op("act", lambda eng, o=vtok[pg][:, :], i=pb[1][0:CH, 0:HKV * 128]:
                     eng.copy(o, i), reads=(bpb[1],), writes=(bvtok[pg],))
                for qh in range(H if ATTN_DBG >= 4 else 0):
                    kvh = qh // 4
                    pq = qh % 2
                    if phase == "B" and ATTN_QH >= 1:
                        pss = pp[pq]
                        P.op("pe", lambda eng, a=kfm[:, kvh * AT + c * CH: kvh * AT + c * CH + CH],
                             b=hq[:, qh * 512 + c * CH: qh * 512 + c * CH + CH],
                             o=pss[0:CH, 0:CH]:
                             eng.matmul(o, a, b, start=True, stop=True),
                             reads=(bkfm[kvh], bqo[qh * 8 + c]), writes=(bpp[pq],))
                        P.op("dve", lambda eng, o=scbf[pq][:, :], p=pss[0:CH, 0:CH],
                             dg=diag[:, qh * CH:(qh + 1) * CH]:
                             eng.tensor_mul(o, p, dg),
                             reads=(bpp[pq], bconst), writes=(bscbf[pq],))
                        P.op("act", lambda eng, o=kvbf[pq][:, :],
                             i=kvsb[:, qh * DH:(qh + 1) * DH]: eng.copy(o, i),
                             reads=(bkv[qh],), writes=(bkvbf[pq],))
                        P.op("dve", lambda eng, o=qtil[pq][:, :],
                             q=hq[:, qh * 512 + c * CH: qh * 512 + c * CH + CH],
                             qd=qdec[:, qh * CH:(qh + 1) * CH]:
                             eng.tensor_mul(o, q, qd),
                             reads=(bqo[qh * 8 + c], bconst), writes=(bqtil[pq],))
                        pso = pp[2 + pq]
                        if ATTN_QH >= 2:
                            P.op("pe", lambda eng, a=vtok[pg][:, kvh * 128:(kvh + 1) * 128],
                                 b=scbf[pq][:, :], o=pso[:, 0:CH],
                                 t=(ATTN_QH < 3):
                                 eng.matmul(o, a, b, start=True, stop=t),
                                 reads=(bvtok[pg], bscbf[pq]), writes=(bpp[2 + pq],))
                        if ATTN_QH >= 3:
                            P.op("pe", lambda eng, a=kvbf[pq][:, :], b=qtil[pq][:, :],
                                 o=pso[:, 0:CH]:
                                 eng.matmul(o, a, b, start=False, stop=True),
                                 reads=(bkvbf[pq], bqtil[pq]), writes=(bpp[2 + pq],))
                        if ATTN_QH >= 2:
                            P.op("act", lambda eng, o=hq[:, qh * 512 + c * CH: qh * 512 + c * CH + CH],
                                 i=pso[:, 0:CH]: eng.copy(o, i),
                                 reads=(bpp[2 + pq],), writes=(bqo[qh * 8 + c],))
                    if ATTN_QH < 4 or phase == "skip":
                        continue
                    P.op("dve", lambda eng, o=ktil[pq][:, :],
                         k=ktok[pg][:, kvh * 128:(kvh + 1) * 128],
                         kd=kdec[:, qh:qh + 1]:
                         eng.tensor_scalar(o, k, kd, None, ALU.mult),
                         reads=(bktok[pg], bconst), writes=(bktil[pq],))
                    psu = pp[4 + pq]
                    P.op("pe", lambda eng, a=ktil[pq][:, :],
                         b=vtok[pg][:, kvh * 128:(kvh + 1) * 128], o=psu[:, 0:DH]:
                         eng.matmul(o, a, b, start=True, stop=True),
                         reads=(bktil[pq], bvtok[pg]), writes=(bpp[4 + pq],))
                    P.op("dve", lambda eng, kv=kvsb[:, qh * DH:(qh + 1) * DH],
                         p=psu[:, 0:DH], bl=blk[qh]:
                         eng.scalar_tensor_tensor(kv, kv, bl, p, ALU.mult, ALU.add),
                         reads=(bkv[qh], bpp[4 + pq]), writes=(bkv[qh],))
            if phase == "B" and ATTN_DBG >= 2 and "o" in ATTN_SUB:
                for db in range(DCH):
                    wi = next_wb()
                    src = wo_in[db].rearrange("c p f -> p c f")
                    P.op("pool", lambda eng,
                         d=wb[wi][:, 0:2048].rearrange("p (c f) -> p c f", f=128), s=src:
                         eng.dma_start(d, s), reads=(), writes=(bwb[wi],), inc=16)
                    psy = pp[4 + db % 2]
                    for eb in range(H):
                        P.op("pe", lambda eng, a=wb[wi][:, eb * 128:(eb + 1) * 128],
                             b=hq[:, eb * 512: eb * 512 + 512], o=psy[:, :],
                             s=(eb == 0), t=(eb == H - 1):
                             eng.matmul(o, a, b, start=s, stop=t),
                             reads=tuple([bwb[wi]] + [bqo[eb * 8 + cc] for cc in range(8)]),
                             writes=(bpp[4 + db % 2],))
                    xap = xsb[:, db * FT: db * FT + 512]
                    P.op("dve", lambda eng, x=xap, p=psy[:, :]:
                         eng.tensor_add(x, x, p),
                         reads=(bpp[4 + db % 2], bx[db]), writes=(bx[db],))
            if phase == "B":
                dst = x2 if not mode.startswith('attn') else yout
                dbufs = bx2 if not mode.startswith('attn') else bout
                d, s = store_x_tile(dst, tc, AT)
                P.op("sync", lambda eng, d=d, s=s: eng.dma_start(d, s),
                     reads=tuple(bx), writes=(dbufs[at],), inc=16)

        def emit_attn():
            P.op("dve", lambda eng: eng.memset(kvsb[:, :], 0.0),
                 writes=tuple(bkv))
            for at in range(4):
                emit_attn_phase("A", at)
            P.op("dve", lambda eng: eng.tensor_scalar(
                kvsb[:, :], kvsb[:, :], msend[:, 0:1], None, ALU.mult),
                reads=tuple(bkv) + (bconst,), writes=tuple(bkv))
            if mode != "attn_nocc":
                P.op("sync", lambda eng: eng.dma_start(cc_in, kvsb[:, :]),
                     reads=tuple(bkv), writes=(bccin,), inc=16)
                P.op("cc", lambda eng: eng.collective_compute(
                    "AllReduce", ALU.add,
                    replica_groups=[[0, 1], [2, 3], [4, 5], [6, 7]],
                    ins=[cc_in], outs=[cc_out]),
                    reads=(bccin,), writes=(bccout,), inc=1)
                P.op("sync", lambda eng: eng.dma_start(kvsb[:, :], cc_out),
                     reads=(bccout,) + tuple(bkv), writes=tuple(bkv), inc=16)
            P.op("dve", lambda eng: eng.tensor_scalar(
                kvsb[:, :], kvsb[:, :], mrecv[:, 0:1], None, ALU.mult),
                reads=tuple(bkv) + (bconst,), writes=tuple(bkv))
            for at in range(4):
                bqo = [rebind(P, bh) for _ in range(H * 8)]
                emit_attn_phase("B", at, bqo=bqo)
                newh = [rebind(P, bqo) for _ in range(FCH)]
                for i in range(FCH):
                    bh[i] = newh[i]

        if mode == "full":
            emit_ffn(xin, x1, [Buf() for _ in range(4)], bx1)
            emit_attn()
            emit_ffn(x2, yout, bx2, bout)
        elif mode == "ffn1":
            emit_ffn(xin, yout, [Buf() for _ in range(4)], bout)
        elif mode in ("attn", "attn_nocc"):
            emit_attn()
        P.op("sync", lambda eng: eng.nop(), reads=tuple(bout))

        P.replay(nc, sems)
    return nc



def _arr(a):
    return np.ascontiguousarray(a, dtype=np.float32)


def _prep_shared(inp):
    rw1 = inp["L0_ffn_routed_w1"]
    sw1 = inp["L0_ffn_shared_w1"]
    rw2 = inp["L0_ffn_routed_w2"]
    sw2 = inp["L0_ffn_shared_w2"]

    def w1_arr(w):
        n = w.shape[0]
        return _arr(np.transpose(w.reshape(n, DCH, 128, FCH, 128), (0, 3, 1, 2, 4)))

    def w2_arr(w):
        n = w.shape[0]
        return _arr(np.transpose(w.reshape(n, FCH, 128, DCH, 128), (0, 3, 1, 2, 4)))

    b1a = np.concatenate([np.asarray(inp["L0_ffn_routed_b1"]),
                          np.asarray(inp["L0_ffn_shared_b1"])], axis=0)
    b1a = _arr(np.transpose(b1a.reshape(NR + NS, FCH, 128), (2, 0, 1))
               .reshape(128, (NR + NS) * FCH))

    cent = np.asarray(inp["L0_ffn_centroids"])
    centT = _arr(cent.T.reshape(DCH, 128, NR))

    wq = np.asarray(inp["L0_attn_wq"])
    wk = np.asarray(inp["L0_attn_wk"])
    wv = np.asarray(inp["L0_attn_wv"])
    wo = np.asarray(inp["L0_attn_wo"])

    def wqT_arr(w, nh):
        wT = w.T
        return _arr(np.transpose(wT.reshape(DCH, 128, nh, 128), (2, 0, 1, 3)))

    woT = wo.T
    woT_arr = _arr(np.transpose(woT.reshape(H, 128, DCH, 128), (2, 0, 1, 3)))

    hs = np.arange(1, H + 1, dtype=np.float64)
    lam = np.exp2(-8.0 * hs / H)
    r = np.arange(CH, dtype=np.float64)
    diff = r[:, None] - r[None, :]
    diagT = np.zeros((CH, H * CH), dtype=np.float64)
    for h in range(H):
        m = np.where(diff >= 0, np.exp(-lam[h] * diff), 0.0)
        diagT[:, h * CH:(h + 1) * CH] = m.T
    qdec = np.zeros((128, H * CH), dtype=np.float64)
    for h in range(H):
        qdec[:, h * CH:(h + 1) * CH] = np.exp(-lam[h] * (r + 1.0))[None, :]
    kdec = np.zeros((CH, H), dtype=np.float64)
    for h in range(H):
        kdec[:, h] = np.exp(-lam[h] * (CH - 1.0 - r))

    eye = np.eye(128, dtype=np.float32)
    rot = np.zeros((128, 128), dtype=np.float32)
    for m in range(64):
        rot[m + 64, m] = -1.0
    for m in range(64, 128):
        rot[m - 64, m] = 1.0
    esel = np.zeros((NR, NR * 128), dtype=np.float32)
    for j in range(NR):
        esel[j, j * 128:(j + 1) * 128] = 1.0

    return dict(
        rw1=w1_arr(rw1), sw1=w1_arr(sw1), rw2=w2_arr(rw2), sw2=w2_arr(sw2),
        b1a=b1a, rb2=_arr(inp["L0_ffn_routed_b2"]), sb2=_arr(inp["L0_ffn_shared_b2"]),
        centT=centT, rbias=_arr(np.asarray(inp["L0_ffn_bias"]).reshape(NR, 1)),
        btile_in=_arr(np.broadcast_to(np.asarray(inp["L0_ffn_bias"])[None, :],
                                      (128, NR))),
        wqT=wqT_arr(wq, H), wkT=wqT_arr(wk, HKV), wvT=wqT_arr(wv, HKV),
        woT=woT_arr,
        diagT=_arr(diagT), qdec=_arr(qdec), kdec=_arr(kdec),
        eye=eye, rotR=rot, esel=esel,
        onesc=np.ones((128, 1), np.float32), onesr=np.ones((1, 128), np.float32),
    )


def _rope_tables(p0):
    d = DH
    inv = 1.0 / (10000.0 ** (np.arange(0, d, 2, dtype=np.float64) / d))
    t = np.arange(p0, p0 + NTOK, dtype=np.float64)
    fr = t[None, :] * np.concatenate([inv, inv])[:, None]
    return _arr(np.cos(fr)), _arr(np.sin(fr))


def kernel(L0_attn_x, L0_attn_wq, L0_attn_wk, L0_attn_wv, L0_attn_wo,
           L0_ffn_shared_w1, L0_ffn_shared_b1, L0_ffn_shared_w2, L0_ffn_shared_b2,
           L0_ffn_routed_w1, L0_ffn_routed_b1, L0_ffn_routed_w2, L0_ffn_routed_b2,
           L0_ffn_centroids, L0_ffn_bias, _mode="full"):
    inp = dict(L0_attn_x=L0_attn_x, L0_attn_wq=L0_attn_wq, L0_attn_wk=L0_attn_wk,
               L0_attn_wv=L0_attn_wv, L0_attn_wo=L0_attn_wo,
               L0_ffn_shared_w1=L0_ffn_shared_w1, L0_ffn_shared_b1=L0_ffn_shared_b1,
               L0_ffn_shared_w2=L0_ffn_shared_w2, L0_ffn_shared_b2=L0_ffn_shared_b2,
               L0_ffn_routed_w1=L0_ffn_routed_w1, L0_ffn_routed_b1=L0_ffn_routed_b1,
               L0_ffn_routed_w2=L0_ffn_routed_w2, L0_ffn_routed_b2=L0_ffn_routed_b2,
               L0_ffn_centroids=L0_ffn_centroids, L0_ffn_bias=L0_ffn_bias)
    shared = _prep_shared(inp)

    x = np.asarray(L0_attn_x, dtype=np.float32).reshape(4 * 4096, D)
    in_maps = []
    for c in range(NCORE):
        xs = _arr(x[c * NTOK:(c + 1) * NTOK, :].T)
        cos, sin = _rope_tables((c % 2) * NTOK)
        even = 1.0 if c % 2 == 0 else 0.0
        m = dict(shared)
        m["xT"] = xs
        m["costab"] = cos
        m["sintab"] = sin
        m["msend"] = np.full((128, 1), even, np.float32)
        m["mrecv"] = np.full((128, 1), 1.0 - even, np.float32)
        in_maps.append(m)

    nc = build_program(mode=_mode)
    res = run_bass_kernel_spmd(nc, in_maps, core_ids=list(range(NCORE)))
    out = np.empty((4 * 4096, D), dtype=np.float32)
    for c in range(NCORE):
        out[c * NTOK:(c + 1) * NTOK, :] = res.results[c]["y"].T
    return out.reshape(4, 4096, D)


# version 04
# speedup vs baseline: 1.2018x; 1.1493x over previous
import contextlib
import math
import os

ATTN_DBG = int(os.environ.get("ATTN_DBG", "4"))
ATTN_SUB = os.environ.get("ATTN_SUB", "kvqor")
ATTN_ROPE = int(os.environ.get("ATTN_ROPE", "5"))
ATTN_QH = int(os.environ.get("ATTN_QH", "4"))

import numpy as np

import concourse.bass as bass
import concourse.mybir as mybir
from concourse.bass_utils import run_bass_kernel_spmd

F32 = mybir.dt.float32
BF16 = mybir.dt.bfloat16
AF = mybir.ActivationFunctionType
ALU = mybir.AluOpType
AX = mybir.AxisListType

D = 2048
DCH = 16
FF = 1408
FCH = 11
NTOK = 2048
FT = 1024
AT = 512
NR = 16
NS = 2
KR = 4
H = 16
HKV = 4
DH = 128
CH = 64
NCORE = 8

POOL_LANES = tuple(f"pool{i}" for i in range(8))
SYNC_LANES = tuple(f"sync{i}" for i in range(4))
ENGINES = ("pe", "act", "dve") + POOL_LANES + SYNC_LANES + ("cc",)
PHYS = {}
for _e in ("pe", "act", "dve"):
    PHYS[_e] = _e
for _e in POOL_LANES:
    PHYS[_e] = "pool"
for _e in SYNC_LANES:
    PHYS[_e] = "sync"
PHYS["cc"] = "pool"


class Buf:
    __slots__ = ("writer", "readers", "init", "psum")

    def __init__(self, init=(), psum=False):
        self.writer = None
        self.readers = []
        self.init = list(init)
        self.psum = psum


def rebind(P, bufs):
    init = []
    for b in bufs:
        if b.writer is not None:
            init.append(b.writer)
        init.extend(b.readers)
        init.extend(b.init)
    return Buf(init=P.reduce_gids(init))


class Op:
    __slots__ = ("gid", "eng", "seq", "fn", "inc", "deps", "after", "marked",
                 "wait_list", "semval")

    def __init__(self, gid, eng, seq, fn, inc, deps, after):
        self.gid = gid
        self.eng = eng
        self.seq = seq
        self.fn = fn
        self.inc = inc
        self.deps = deps
        self.after = after
        self.marked = False
        self.wait_list = []
        self.semval = 0


class Prog:
    def __init__(self):
        self.ops = []
        self.eng_ops = {e: [] for e in ENGINES}
        self.clock = {e: {e2: -1 for e2 in ENGINES} for e in ENGINES}
        self.rr = {"pool": 0, "sync": 0}
        self.last_on_lane = {}

    def reduce_gids(self, gids):
        best = {}
        for g in gids:
            o = self.ops[g]
            if o.eng not in best or self.ops[best[o.eng]].seq < o.seq:
                best[o.eng] = g
        return list(best.values())

    def op(self, eng, fn, reads=(), writes=(), inc=1):
        extra = ()
        if eng in ("pool", "sync"):
            if inc == 16:
                lanes = POOL_LANES if eng == "pool" else SYNC_LANES
                lane = lanes[self.rr[eng] % len(lanes)]
                self.rr[eng] += 1
                if lane in self.last_on_lane:
                    extra = (self.last_on_lane[lane],)
                self.last_on_lane[lane] = len(self.ops)
                eng = lane
            else:
                eng = (POOL_LANES if eng == "pool" else SYNC_LANES)[0]
        gid = len(self.ops)
        seq = len(self.eng_ops[eng])
        dep_gids = set(extra)
        for b in reads:
            if b.writer is not None:
                dep_gids.add(b.writer)
            dep_gids.update(b.init)
            if b.psum and eng in ("act", "dve"):
                for g in b.readers:
                    if self.ops[g].eng in ("act", "dve") and self.ops[g].eng != eng:
                        dep_gids.add(g)
        for b in writes:
            if b.writer is not None:
                dep_gids.add(b.writer)
            dep_gids.update(b.readers)
            dep_gids.update(b.init)
        cl = self.clock[eng]
        deps = []
        for g in sorted(self.reduce_gids(dep_gids)):
            o = self.ops[g]
            if o.eng == "pe" and eng == "pe":
                continue
            if cl[o.eng] >= o.seq:
                continue
            deps.append(g)
            for e2 in ENGINES:
                if o.after[e2] > cl[e2]:
                    cl[e2] = o.after[e2]
        after = dict(cl)
        after[eng] = seq
        op = Op(gid, eng, seq, fn, inc, deps, after)
        self.ops.append(op)
        self.eng_ops[eng].append(op)
        for b in reads:
            b.readers.append(gid)
        for b in writes:
            b.writer = gid
            b.readers = []
        return gid

    def finalize(self):
        for op in self.ops:
            if op.inc == 16:
                op.marked = True
            for g in op.deps:
                self.ops[g].marked = True
        cum = {e: 0 for e in ENGINES}
        for op in self.ops:
            if op.marked:
                cum[op.eng] += op.inc
                op.semval = cum[op.eng]
        for op in self.ops:
            op.wait_list = [(self.ops[g].eng, self.ops[g].semval)
                            for g in op.deps]
        self.n_marked = sum(1 for o in self.ops if o.marked)

    def replay(self, nc, sems):
        self.finalize()
        with nc.Block() as block:
            def mk(phys):
                ops = sorted((o for o in self.ops if PHYS[o.eng] == phys),
                             key=lambda o: o.gid)

                def body(eng):
                    for op in ops:
                        for (pe_eng, val) in op.wait_list:
                            eng.wait_ge(sems[pe_eng], val)
                        ins = op.fn(eng)
                        if op.marked and ins is not None:
                            ins.then_inc(sems[op.eng], op.inc)
                return body
            block.tensor(mk("pe"))
            block.scalar(mk("act"))
            block.vector(mk("dve"))
            block.gpsimd(mk("pool"))
            block.sync(mk("sync"))



def build_program(mode="full"):
    nc = bass.Bass("TRN2", target_bir_lowering=False, debug=False)

    xin = nc.dram_tensor("xT", [D, NTOK], F32, kind="ExternalInput").ap()
    yout = nc.dram_tensor("y", [D, NTOK], F32, kind="ExternalOutput").ap()
    x1 = nc.dram_tensor("x1d", [D, NTOK], F32).ap()
    x2 = nc.dram_tensor("x2d", [D, NTOK], F32).ap()
    cc_in = nc.dram_tensor("cc_in", [128, H * DH], F32).ap()
    cc_out = nc.dram_tensor("cc_out", [128, H * DH], F32).ap()

    rw1 = nc.dram_tensor("rw1", [NR, FCH, DCH, 128, 128], F32, kind="ExternalInput").ap()
    sw1 = nc.dram_tensor("sw1", [NS, FCH, DCH, 128, 128], F32, kind="ExternalInput").ap()
    rw2 = nc.dram_tensor("rw2", [NR, DCH, FCH, 128, 128], F32, kind="ExternalInput").ap()
    sw2 = nc.dram_tensor("sw2", [NS, DCH, FCH, 128, 128], F32, kind="ExternalInput").ap()
    b1in = nc.dram_tensor("b1a", [128, (NR + NS) * FCH], F32, kind="ExternalInput").ap()
    rb2 = nc.dram_tensor("rb2", [NR, D], F32, kind="ExternalInput").ap()
    sb2 = nc.dram_tensor("sb2", [NS, D], F32, kind="ExternalInput").ap()
    centin = nc.dram_tensor("centT", [DCH, 128, NR], F32, kind="ExternalInput").ap()
    biasin = nc.dram_tensor("rbias", [NR, 1], F32, kind="ExternalInput").ap()
    wq_in = nc.dram_tensor("wqT", [H, DCH, 128, 128], F32, kind="ExternalInput").ap()
    wk_in = nc.dram_tensor("wkT", [HKV, DCH, 128, 128], F32, kind="ExternalInput").ap()
    wv_in = nc.dram_tensor("wvT", [HKV, DCH, 128, 128], F32, kind="ExternalInput").ap()
    wo_in = nc.dram_tensor("woT", [DCH, H, 128, 128], F32, kind="ExternalInput").ap()
    cosin = nc.dram_tensor("costab", [128, NTOK], F32, kind="ExternalInput").ap()
    sinin = nc.dram_tensor("sintab", [128, NTOK], F32, kind="ExternalInput").ap()
    diagin = nc.dram_tensor("diagT", [CH, H * CH], F32, kind="ExternalInput").ap()
    qdecin = nc.dram_tensor("qdec", [128, H * CH], F32, kind="ExternalInput").ap()
    kdecin = nc.dram_tensor("kdec", [CH, H], F32, kind="ExternalInput").ap()
    eyein = nc.dram_tensor("eye", [128, 128], F32, kind="ExternalInput").ap()
    rotin = nc.dram_tensor("rotR", [128, 128], F32, kind="ExternalInput").ap()
    eselin = nc.dram_tensor("esel", [NR, NR * 128], F32, kind="ExternalInput").ap()
    onescin = nc.dram_tensor("onesc", [128, 1], F32, kind="ExternalInput").ap()
    onesrin = nc.dram_tensor("onesr", [1, 128], F32, kind="ExternalInput").ap()
    msendin = nc.dram_tensor("msend", [128, 1], F32, kind="ExternalInput").ap()
    btilein = nc.dram_tensor("btile_in", [128, NR], F32, kind="ExternalInput").ap()
    mrecvin = nc.dram_tensor("mrecv", [128, 1], F32, kind="ExternalInput").ap()

    lam = [2.0 ** (-8.0 * (h + 1) / 16.0) for h in range(H)]
    blk = [math.exp(-CH * lam[h]) for h in range(H)]

    es = contextlib.ExitStack()
    ec = es.enter_context
    with es:
        xsb = ec(nc.sbuf_tensor("xsb", [128, DCH * FT], F32))
        xn = ec(nc.sbuf_tensor("xn", [128, DCH * FT], BF16))
        hq = ec(nc.sbuf_tensor("hq", [128, FCH * FT], BF16))
        kfm = ec(nc.sbuf_tensor("kfm", [128, HKV * AT], BF16))
        vfm = ec(nc.sbuf_tensor("vfm", [128, HKV * AT], BF16))
        ktok = [ec(nc.sbuf_tensor(f"ktok{i}", [CH, HKV * DH], BF16)) for i in range(2)]
        vtok = [ec(nc.sbuf_tensor(f"vtok{i}", [CH, HKV * DH], BF16)) for i in range(2)]
        kvsb = ec(nc.sbuf_tensor("kvsb", [128, H * DH], F32))
        kvbf = [ec(nc.sbuf_tensor(f"kvbf{i}", [128, DH], BF16)) for i in range(2)]
        qtil = [ec(nc.sbuf_tensor(f"qtil{i}", [128, CH], BF16)) for i in range(2)]
        ktil = [ec(nc.sbuf_tensor(f"ktil{i}", [CH, DH], BF16)) for i in range(2)]
        scbf = [ec(nc.sbuf_tensor(f"scbf{i}", [CH, CH], BF16)) for i in range(2)]
        wb = [ec(nc.sbuf_tensor(f"wb{i}", [128, 2048], BF16)) for i in range(3)]
        gbf = ec(nc.sbuf_tensor("gbf16", [128, FT], BF16))
        scr = [ec(nc.sbuf_tensor(f"scr{i}", [128, 512], F32)) for i in range(4)]
        rows = ec(nc.sbuf_tensor("rows", [1, 4 * 512], F32))
        gT = ec(nc.sbuf_tensor("gT", [NR, FT], BF16))
        gtok_o = ec(nc.sbuf_tensor("gtok_o", [128, NR], F32))
        gtok_w = ec(nc.sbuf_tensor("gtok_w", [128, NR], F32))
        gtmp = ec(nc.sbuf_tensor("gtmp", [128, NR], F32))
        gm = ec(nc.sbuf_tensor("gm", [128, 8], F32))
        gbf16t = ec(nc.sbuf_tensor("gbf16t", [128, NR], BF16))
        onesc = ec(nc.sbuf_tensor("onesc_s", [128, 1], F32))
        onesr = ec(nc.sbuf_tensor("onesr_s", [1, 128], F32))
        eye_b = ec(nc.sbuf_tensor("eye_b", [128, 128], BF16))
        eye_f = ec(nc.sbuf_tensor("eye_f", [128, 128], F32))
        rot_b = ec(nc.sbuf_tensor("rot_b", [128, 128], BF16))
        b1sb = ec(nc.sbuf_tensor("b1sb", [128, (NR + NS) * FCH], F32))
        b2r_s = ec(nc.sbuf_tensor("b2r_s", [NR, D], BF16))
        b2s_s = ec(nc.sbuf_tensor("b2s_s", [NS, D], BF16))
        ones2 = ec(nc.sbuf_tensor("ones2", [NS, 512], BF16))
        esel = ec(nc.sbuf_tensor("esel_s", [NR, NR * 128], BF16))
        cent = ec(nc.sbuf_tensor("cent_s", [128, DCH * NR], BF16))
        biasr = ec(nc.sbuf_tensor("biasr", [NR, 1], F32))
        kdec = ec(nc.sbuf_tensor("kdec_s", [CH, H], F32))
        diag = ec(nc.sbuf_tensor("diag_s", [CH, H * CH], BF16))
        qdec = ec(nc.sbuf_tensor("qdec_s", [128, H * CH], BF16))
        msend = ec(nc.sbuf_tensor("msend_s", [128, 1], F32))
        mrecv = ec(nc.sbuf_tensor("mrecv_s", [128, 1], F32))
        costl = ec(nc.sbuf_tensor("costl", [128, 512], F32))
        epsc = ec(nc.sbuf_tensor("epsc", [1, 1], F32))
        sbf = [ec(nc.sbuf_tensor(f"sbf{i}", [128, 512], BF16)) for i in range(2)]
        onescb = ec(nc.sbuf_tensor("onescb", [128, 1], BF16))
        gall = ec(nc.sbuf_tensor("gall", [128, 8 * NR], BF16))
        btile = ec(nc.sbuf_tensor("btile", [128, NR], F32))
        gtok2_o = ec(nc.sbuf_tensor("gtok2_o", [128, NR], F32))
        gtok2_w = ec(nc.sbuf_tensor("gtok2_w", [128, NR], F32))
        gtmp2 = ec(nc.sbuf_tensor("gtmp2", [128, NR], F32))
        gm2 = ec(nc.sbuf_tensor("gm2", [128, 8], F32))
        sintl = ec(nc.sbuf_tensor("sintl", [128, 512], F32))
        pp = [ec(nc.psum_tensor(f"pp{i}", [128, 512], F32)) for i in range(6)]
        pb = [ec(nc.psum_tensor(f"pbf{i}", [128, 1024], BF16)) for i in range(2)]

        sems = {e: ec(nc.semaphore(f"sem_{e}")) for e in ENGINES}

        P = Prog()

        bx = [Buf() for _ in range(DCH)]
        bxn = [Buf() for _ in range(DCH)]
        bh = [Buf() for _ in range(FCH)]
        bkfm = [Buf() for _ in range(HKV)]
        bvfm = [Buf() for _ in range(HKV)]
        bktok = [Buf(), Buf()]
        bvtok = [Buf(), Buf()]
        bkv = [Buf() for _ in range(H)]
        bkvbf = [Buf(), Buf()]
        bqtil = [Buf(), Buf()]
        bktil = [Buf(), Buf()]
        bscbf = [Buf(), Buf()]
        bwb = [Buf() for _ in range(3)]
        bgbf = Buf()
        bscr = [Buf() for _ in range(4)]
        brows = [Buf() for _ in range(4)]
        bgT = [Buf(), Buf()]
        bgtok_o = [Buf(), Buf()]
        bgtok_w = [Buf(), Buf()]
        bgtmp = [Buf(), Buf()]
        bgm = [[Buf() for _ in range(8)] for _ in range(2)]
        bgall = [Buf() for _ in range(8)]
        bsbf = [Buf(), Buf()]
        bconst = Buf()
        bpp = [Buf(psum=True) for _ in range(6)]
        bpb = [Buf(psum=True), Buf(psum=True)]
        bcos = Buf(); bsin = Buf()
        bx1 = [Buf() for _ in range(4)]
        bx2 = [Buf() for _ in range(4)]
        bccin = Buf(); bccout = Buf()
        bout = [Buf() for _ in range(4)]

        wb_ctr = [0]

        def next_wb():
            i = wb_ctr[0] % 3
            wb_ctr[0] += 1
            return i

        def ld(engine, dst_ap, src_ap, buf):
            P.op(engine, lambda eng, d=dst_ap, s=src_ap: eng.dma_start(d, s),
                 reads=(), writes=(buf,), inc=16)

        ld("sync", onesc[:], onescin[:], bconst)
        ld("sync", onesr[:], onesrin[:], bconst)
        ld("sync", eye_f[:], eyein[:], bconst)
        ld("sync", b1sb[:], b1in[:], bconst)
        ld("sync", biasr[:], biasin[:], bconst)
        ld("sync", kdec[:], kdecin[:], bconst)
        ld("sync", msend[:], msendin[:], bconst)
        ld("sync", mrecv[:], mrecvin[:], bconst)
        ld("pool", eye_b[:], eyein[:], bconst)
        ld("pool", rot_b[:], rotin[:], bconst)
        ld("pool", b2r_s[:], rb2[:], bconst)
        ld("pool", b2s_s[:], sb2[:], bconst)
        ld("pool", esel[:], eselin[:], bconst)
        ld("pool", cent[:].rearrange("p (c e) -> p c e", e=NR),
           centin.rearrange("c p e -> p c e"), bconst)
        ld("pool", diag[:], diagin[:], bconst)
        ld("pool", qdec[:], qdecin[:], bconst)
        P.op("dve", lambda eng: eng.memset(ones2[:], 1.0), writes=(bconst,))
        ld("pool", onescb[:], onescin[:], bconst)
        ld("sync", btile[:], btilein[:], bconst)
        P.op("dve", lambda eng: eng.memset(epsc[:], 1e-5), writes=(bconst,))

        def emit_ln(xsrc3, xdst_bufs_dram, tcols, width, update_x):
            nsub = width // 512
            for sc in range(nsub):
                c0 = tcols + sc * 512
                ps1 = pp[4]; ps2 = pp[5]
                for ch in range(DCH):
                    xap = xsb[:, ch * FT + c0: ch * FT + c0 + 512]
                    xb = sbf[ch % 2]
                    P.op("act", lambda eng, o=xb[:], i=xap: eng.copy(o, i),
                         reads=(bx[ch],), writes=(bsbf[ch % 2],))
                    P.op("pe", lambda eng, o=ps1[0:1, :], b=xb[:],
                         s=(ch == 0), t=(ch == DCH - 1):
                         eng.matmul(o, onescb[:], b, start=s, stop=t),
                         reads=(bconst, bsbf[ch % 2]), writes=(bpp[4],))
                    sq = scr[ch % 2]
                    P.op("act", lambda eng, o=sq[0:128, 0:512], i=xap:
                         eng.activation(o, i, AF.Square),
                         reads=(bx[ch],), writes=(bscr[ch % 2],))
                    P.op("pe", lambda eng, o=ps2[0:1, :], a=onesc[:], b=sq[:],
                         s=(ch == 0), t=(ch == DCH - 1):
                         eng.matmul(o, a, b, start=s, stop=t),
                         reads=(bconst, bscr[ch % 2]), writes=(bpp[5],))
                r0 = rows[:, 0:512]; r1 = rows[:, 512:1024]
                r2 = rows[:, 1024:1536]; r3 = rows[:, 1536:2048]
                P.op("dve", lambda eng: eng.tensor_copy(r0, ps1[0:1, :]),
                     reads=(bpp[4],), writes=(brows[0],))
                P.op("dve", lambda eng: eng.tensor_copy(r1, ps2[0:1, :]),
                     reads=(bpp[5],), writes=(brows[1],))
                inv = 1.0 / D
                P.op("dve", lambda eng: eng.tensor_scalar_mul(r0, r0, inv),
                     reads=(brows[0],), writes=(brows[0],))
                P.op("dve", lambda eng: eng.tensor_scalar_mul(r1, r1, inv),
                     reads=(brows[1],), writes=(brows[1],))
                P.op("dve", lambda eng: eng.tensor_mul(r2, r0, r0),
                     reads=(brows[0],), writes=(brows[2],))
                P.op("dve", lambda eng: eng.tensor_sub(r2, r1, r2),
                     reads=(brows[1], brows[2]), writes=(brows[2],))
                P.op("act", lambda eng: eng.activation(r2, r2, AF.Sqrt,
                                                       bias=epsc[0:1, 0:1]),
                     reads=(brows[2], bconst), writes=(brows[2],))
                P.op("dve", lambda eng: eng.reciprocal(r2, r2),
                     reads=(brows[2],), writes=(brows[2],))
                P.op("dve", lambda eng: eng.scalar_tensor_tensor(
                    r3, r2, -1.0, r0, ALU.mult, ALU.mult),
                    reads=(brows[2], brows[0]), writes=(brows[3],))
                psA = pp[4]; psB = pp[5]
                P.op("pe", lambda eng: eng.matmul(psA[:, :], onesr[:], r2,
                                                  start=True, stop=True),
                     reads=(bconst, brows[2]), writes=(bpp[4],))
                P.op("pe", lambda eng: eng.matmul(psB[:, :], onesr[:], r3,
                                                  start=True, stop=True),
                     reads=(bconst, brows[3]), writes=(bpp[5],))
                for ch in range(DCH):
                    xap = xsb[:, ch * FT + c0: ch * FT + c0 + 512]
                    xnap = xn[:, ch * FT + c0: ch * FT + c0 + 512]
                    t = scr[2 + ch % 2]
                    P.op("dve", lambda eng, t=t[:], x=xap: eng.tensor_mul(
                        t, x, psA[:, :]),
                        reads=(bx[ch], bpp[4]), writes=(bscr[2 + ch % 2],))
                    P.op("dve", lambda eng, t=t[:], o=xnap: eng.tensor_add(
                        o, t, psB[:, :]),
                        reads=(bscr[2 + ch % 2], bpp[5]), writes=(bxn[ch],))
                    if update_x:
                        P.op("dve", lambda eng, x=xap, o=xnap:
                             eng.scalar_tensor_tensor(x, o, 0.5, x,
                                                      ALU.mult, ALU.add),
                             reads=(bxn[ch], bx[ch]), writes=(bx[ch],))

        def emit_gating_scores(tcols):
            for sc in range(2):
                c0 = tcols + sc * 512
                pss = pp[4]
                for ch in range(DCH):
                    P.op("pe", lambda eng, a=cent[:, ch * NR:(ch + 1) * NR],
                         b=xn[:, ch * FT + c0: ch * FT + c0 + 512],
                         st=(ch == 0), sp=(ch == DCH - 1):
                         eng.matmul(pss[0:NR, :], a, b, start=st, stop=sp),
                         reads=(bconst, bxn[ch]), writes=(bpp[4],))
                sc_scores = scr[3][0:NR, :]
                P.op("act", lambda eng: eng.activation(sc_scores, pss[0:NR, :],
                                                       AF.Sigmoid),
                     reads=(bpp[4],), writes=(bscr[3],))
                P.op("dve", lambda eng: eng.tensor_scalar_add(
                    sc_scores, sc_scores, biasr[:, 0:1]),
                    reads=(bscr[3], bconst), writes=(bscr[3],))
                for mcl in range(4):
                    mc = sc * 4 + mcl
                    m0 = mcl * 128
                    par = mc % 2
                    pst = pp[4 + par]
                    P.op("pe", lambda eng, i=scr[3][0:NR, m0:m0 + 128], o=pst[:, 0:NR]:
                         eng.transpose(o, i, eye_f[0:NR, 0:NR]),
                         reads=(bscr[3], bconst), writes=(bpp[4 + par],))
                    go = (gtok_o, gtok2_o)[par]; gw = (gtok_w, gtok2_w)[par]
                    gt = (gtmp, gtmp2)[par]; gmm = (gm, gm2)[par]
                    bgo = bgtok_o[par]; bgw = bgtok_w[par]
                    bgt = bgtmp[par]; bgmm = bgm[par]
                    P.op("dve", lambda eng, o=go[:], i=pst[:, 0:NR]:
                         eng.tensor_copy(o, i),
                         reads=(bpp[4 + par],), writes=(bgo,))
                    P.op("dve", lambda eng, o=gw[:], i=go[:]: eng.tensor_copy(o, i),
                         reads=(bgo,), writes=(bgw,))
                    for r in range(KR):
                        P.op("dve", lambda eng, m=gmm[:, r:r + 1], i=gw[:]:
                             eng.reduce_max(m, i, axis=AX.X),
                             reads=(bgw,), writes=(bgmm[r],))
                        if r < KR - 1:
                            P.op("dve", lambda eng, m=gmm[:, r:r + 1], o=gt[:], i=gw[:]:
                                 eng.tensor_scalar(o, i, m, None, ALU.is_ge),
                                 reads=(bgw, bgmm[r]), writes=(bgt,))
                            P.op("dve", lambda eng, o=gw[:], t=gt[:]:
                                 eng.scalar_tensor_tensor(o, t, -1e30, o,
                                                          ALU.mult, ALU.add),
                                 reads=(bgt, bgw), writes=(bgw,))
                    P.op("dve", lambda eng, o=gt[:], i=go[:], m=gmm[:, 3:4]:
                         eng.tensor_scalar(o, i, m, None, ALU.is_ge),
                         reads=(bgo, bgmm[3]), writes=(bgt,))
                    P.op("dve", lambda eng, o=go[:], t=gt[:]:
                         eng.tensor_mul(o, o, t),
                         reads=(bgo, bgt), writes=(bgo,))
                    P.op("dve", lambda eng, m=gmm[:, 4:5], i=go[:]:
                         eng.reduce_sum(m, i, axis=AX.X),
                         reads=(bgo,), writes=(bgmm[4],))
                    P.op("dve", lambda eng, m=gmm[:, 4:5]:
                         eng.tensor_scalar_max(m, m, 1e-8),
                         reads=(bgmm[4],), writes=(bgmm[4],))
                    P.op("dve", lambda eng, o=gmm[:, 5:6], i=gmm[:, 4:5]:
                         eng.reciprocal(o, i),
                         reads=(bgmm[4],), writes=(bgmm[5],))
                    P.op("dve", lambda eng, o=gall[:, mc * NR:(mc + 1) * NR],
                         i=go[:], m=gmm[:, 5:6]:
                         eng.tensor_scalar(o, i, m, None, ALU.mult),
                         reads=(bgo, bgmm[5]), writes=(bgall[mc],))

        def emit_gating_transposes(tcols):
            for mc in range(8):
                sc = (mc * 128) // 512
                m0 = (mc * 128) % 512
                P.op("pe", lambda eng, i=gall[:, mc * NR:(mc + 1) * NR]:
                     eng.transpose(pb[0][0:NR, 0:128], i, eye_b[:]),
                     reads=(bgall[mc], bconst), writes=(bpb[0],))
                P.op("act", lambda eng, o=gT[:, sc * 512 + m0: sc * 512 + m0 + 128]:
                     eng.copy(o, pb[0][0:NR, 0:128]),
                     reads=(bpb[0],), writes=(bgT[sc],))

        def emit_expert(kind, j, tcols):
            w1ap = (sw1 if kind == "s" else rw1)
            w2ap = (sw2 if kind == "s" else rw2)
            bcol0 = (NR * FCH + j * FCH) if kind == "s" else j * FCH
            if kind == "r":
                for sc in range(2):
                    psg = pp[2 + sc]
                    P.op("pe", lambda eng, a=esel[:, j * 128:(j + 1) * 128],
                         g=gT[:, sc * 512:(sc + 1) * 512], o=psg[:, :]:
                         eng.matmul(o, a, g, start=True, stop=True),
                         reads=(bconst, bgT[sc]), writes=(bpp[2 + sc],))
                    P.op("act", lambda eng, o=gbf[:, sc * 512:(sc + 1) * 512],
                         i=psg[:, :]: eng.copy(o, i),
                         reads=(bpp[2 + sc],), writes=(bgbf,))
            for fb in range(FCH):
                wi = next_wb()
                src = w1ap[j, fb].rearrange("c p f -> p c f")
                P.op("pool", lambda eng,
                     d=wb[wi][:, 0:2048].rearrange("p (c f) -> p c f", f=128), s=src:
                     eng.dma_start(d, s), reads=(), writes=(bwb[wi],), inc=16)
                for sc in range(2):
                    psh = pp[sc]
                    c0 = tcols + sc * 512
                    for ch in range(DCH):
                        P.op("pe", lambda eng, a=wb[wi][:, ch * 128:(ch + 1) * 128],
                             b=xn[:, ch * FT + c0: ch * FT + c0 + 512],
                             o=psh[:, :], s=(ch == 0), t=(ch == DCH - 1):
                             eng.matmul(o, a, b, start=s, stop=t),
                             reads=(bwb[wi], bxn[ch]), writes=(bpp[sc],))
                    hap = hq[:, fb * FT + sc * 512: fb * FT + sc * 512 + 512]
                    P.op("act", lambda eng, o=hap, i=psh[:, :],
                         b=b1sb[:, bcol0 + fb: bcol0 + fb + 1]:
                         eng.activation(o, i, AF.Relu, bias=b),
                         reads=(bpp[sc], bconst), writes=(bh[fb],))
                    if kind == "r":
                        P.op("dve", lambda eng, o=hap,
                             g=gbf[:, sc * 512:(sc + 1) * 512]:
                             eng.tensor_mul(o, o, g),
                             reads=(bh[fb], bgbf), writes=(bh[fb],))
            for db in range(DCH):
                wi = next_wb()
                src = w2ap[j, db].rearrange("c p f -> p c f")
                P.op("pool", lambda eng,
                     d=wb[wi][:, 0:FCH * 128].rearrange("p (c f) -> p c f", f=128), s=src:
                     eng.dma_start(d, s), reads=(), writes=(bwb[wi],), inc=16)
                for sc in range(2):
                    psy = pp[2 + (db % 2) * 2 + sc]
                    bpsy = bpp[2 + (db % 2) * 2 + sc]
                    c0 = tcols + sc * 512
                    first = True
                    if kind == "s" and j == 0:
                        P.op("pe", lambda eng, a=b2s_s[:, db * 128:(db + 1) * 128],
                             o=psy[:, :]: eng.matmul(o, a, ones2[:, :],
                                                     start=True, stop=False),
                             reads=(bconst,), writes=(bpsy,))
                        first = False
                    if kind == "r" and j == 0:
                        P.op("pe", lambda eng, a=b2r_s[:, db * 128:(db + 1) * 128],
                             g=gT[:, sc * 512:(sc + 1) * 512], o=psy[:, :]:
                             eng.matmul(o, a, g, start=True, stop=False),
                             reads=(bconst, bgT[sc]), writes=(bpsy,))
                        first = False
                    for fb in range(FCH):
                        hap = hq[:, fb * FT + sc * 512: fb * FT + sc * 512 + 512]
                        P.op("pe", lambda eng, a=wb[wi][:, fb * 128:(fb + 1) * 128],
                             b=hap, o=psy[:, :], s=(first and fb == 0),
                             t=(fb == FCH - 1):
                             eng.matmul(o, a, b, start=s, stop=t),
                             reads=(bwb[wi], bh[fb]), writes=(bpsy,))
                    xap = xsb[:, db * FT + c0: db * FT + c0 + 512]
                    P.op("dve", lambda eng, x=xap, p=psy[:, :]:
                         eng.scalar_tensor_tensor(x, p, 0.5, x, ALU.mult, ALU.add),
                         reads=(bpsy, bx[db]), writes=(bx[db],))

        def load_x_tile(src, tcols, width):
            src3 = src.rearrange("(c p) t -> p c t", p=128)
            dst3 = xsb[:, :].rearrange("p (c t) -> p c t", t=FT)
            return (dst3[:, :, 0:width], src3[:, :, tcols:tcols + width])

        def store_x_tile(dst, tcols, width):
            src3 = xsb[:, :].rearrange("p (c t) -> p c t", t=FT)
            dst3 = dst.rearrange("(c p) t -> p c t", p=128)
            return (dst3[:, :, tcols:tcols + width], src3[:, :, 0:width])

        def emit_ffn(src, dst, src_bufs, dst_bufs):
            for tile in range(2):
                tc = tile * FT
                d, s = load_x_tile(src, tc, FT)
                P.op("sync", lambda eng, d=d, s=s: eng.dma_start(d, s),
                     reads=tuple(src_bufs[2 * tile: 2 * tile + 2]),
                     writes=tuple(bx), inc=16)
                emit_ln(None, None, 0, FT, update_x=True)
                emit_gating_scores(0)
                for (kind, j) in [("s", 0), ("s", 1)]:
                    emit_expert(kind, j, 0)
                emit_gating_transposes(0)
                for (kind, j) in [("r", jj) for jj in range(NR)]:
                    emit_expert(kind, j, 0)
                d, s = store_x_tile(dst, tc, FT)
                P.op("sync", lambda eng, d=d, s=s: eng.dma_start(d, s),
                     reads=tuple(bx),
                     writes=tuple(dst_bufs[2 * tile: 2 * tile + 2]), inc=16)

        def proj_rope(w_ap, nheads, dst_sb, dst_bufs, do_rope, at):
            for hh in range(nheads):
                wi = next_wb()
                src = w_ap[hh].rearrange("c p f -> p c f")
                P.op("pool", lambda eng,
                     d=wb[wi][:, 0:2048].rearrange("p (c f) -> p c f", f=128), s=src:
                     eng.dma_start(d, s), reads=(), writes=(bwb[wi],), inc=16)
                psp = pp[hh % 2]
                for ch in range(DCH):
                    P.op("pe", lambda eng, a=wb[wi][:, ch * 128:(ch + 1) * 128],
                         b=xn[:, ch * FT: ch * FT + 512], o=psp[:, :],
                         s=(ch == 0), t=(ch == DCH - 1):
                         eng.matmul(o, a, b, start=s, stop=t),
                         reads=(bwb[wi], bxn[ch]), writes=(bpp[hh % 2],))
                if do_rope:
                    kraw = hq[:, 8 * FT: 8 * FT + 512]
                    t0 = scr[hh % 2]; t1 = scr[2 + hh % 2]
                    P.op("dve", lambda eng, t=t0[:], p=psp[:, :]:
                         eng.tensor_mul(t, p, costl[:]),
                         reads=(bpp[hh % 2], bcos), writes=(bscr[hh % 2],))
                    if ATTN_ROPE >= 2:
                        P.op("act", lambda eng, i=psp[:, :]: eng.copy(kraw, i),
                             reads=(bpp[hh % 2],), writes=(bh[8],))
                    if ATTN_ROPE >= 3:
                        psr = pp[2 + hh % 2]
                        P.op("pe", lambda eng, o=psr[:, :]:
                             eng.matmul(o, rot_b[:], kraw, start=True, stop=True),
                             reads=(bconst, bh[8]), writes=(bpp[2 + hh % 2],))
                    if ATTN_ROPE >= 4:
                        P.op("dve", lambda eng, t=t1[:], p=psr[:, :]:
                             eng.tensor_mul(t, p, sintl[:]),
                             reads=(bpp[2 + hh % 2], bsin), writes=(bscr[2 + hh % 2],))
                    if ATTN_ROPE >= 5:
                        P.op("dve", lambda eng, o=dst_sb[:, hh * 512:(hh + 1) * 512],
                             a=t0[:], b=t1[:]: eng.tensor_add(o, a, b),
                             reads=(bscr[hh % 2], bscr[2 + hh % 2]),
                             writes=tuple(dst_bufs(hh)))
                    else:
                        P.op("dve", lambda eng, o=dst_sb[:, hh * 512:(hh + 1) * 512],
                             a=t0[:]: eng.tensor_copy(o, a),
                             reads=(bscr[hh % 2],),
                             writes=tuple(dst_bufs(hh)))
                else:
                    P.op("act", lambda eng, o=dst_sb[:, hh * 512:(hh + 1) * 512],
                         i=psp[:, :]: eng.copy(o, i),
                         reads=(bpp[hh % 2],), writes=tuple(dst_bufs(hh)))

        def emit_attn_phase(phase, at, bqo=None):
            tc = at * AT
            d, s = load_x_tile(x1 if not mode.startswith('attn') else xin, tc, AT)
            P.op("sync", lambda eng, d=d, s=s: eng.dma_start(d, s),
                 reads=(bx1[at],) if not mode.startswith('attn') else (),
                 writes=tuple(bx), inc=16)
            emit_ln(None, None, 0, AT, update_x=False)
            P.op("sync", lambda eng: eng.dma_start(costl[:], cosin[:, tc:tc + AT]),
                 reads=(bcos,), writes=(bcos,), inc=16)
            P.op("sync", lambda eng: eng.dma_start(sintl[:], sinin[:, tc:tc + AT]),
                 reads=(bsin,), writes=(bsin,), inc=16)
            if ATTN_DBG >= 2:
                rope_on = "r" in ATTN_SUB
                if "k" in ATTN_SUB:
                    proj_rope(wk_in, HKV, kfm, lambda h: (bkfm[h],), rope_on, at)
                if "v" in ATTN_SUB:
                    proj_rope(wv_in, HKV, vfm, lambda h: (bvfm[h],), False, at)
                if phase == "B" and "q" in ATTN_SUB:
                    proj_rope(wq_in, H, hq,
                              lambda h: tuple(bqo[h * 8 + c] for c in range(8)), rope_on, at)
            for c in range(8 if ATTN_DBG >= 3 else 0):
                pg = c % 2
                for kvh in range(HKV):
                    P.op("pe", lambda eng, i=kfm[:, kvh * AT + c * CH: kvh * AT + c * CH + CH],
                         o=pb[0][0:CH, kvh * 128:(kvh + 1) * 128]:
                         eng.transpose(o, i, eye_b[:]),
                         reads=(bkfm[kvh], bconst), writes=(bpb[0],))
                    P.op("pe", lambda eng, i=vfm[:, kvh * AT + c * CH: kvh * AT + c * CH + CH],
                         o=pb[1][0:CH, kvh * 128:(kvh + 1) * 128]:
                         eng.transpose(o, i, eye_b[:]),
                         reads=(bvfm[kvh], bconst), writes=(bpb[1],))
                P.op("act", lambda eng, o=ktok[pg][:, :], i=pb[0][0:CH, 0:HKV * 128]:
                     eng.copy(o, i), reads=(bpb[0],), writes=(bktok[pg],))
                P.op("act", lambda eng, o=vtok[pg][:, :], i=pb[1][0:CH, 0:HKV * 128]:
                     eng.copy(o, i), reads=(bpb[1],), writes=(bvtok[pg],))
                for qh in range(H if ATTN_DBG >= 4 else 0):
                    kvh = qh // 4
                    pq = qh % 2
                    if phase == "B" and ATTN_QH >= 1:
                        pss = pp[pq]
                        P.op("pe", lambda eng, a=kfm[:, kvh * AT + c * CH: kvh * AT + c * CH + CH],
                             b=hq[:, qh * 512 + c * CH: qh * 512 + c * CH + CH],
                             o=pss[0:CH, 0:CH]:
                             eng.matmul(o, a, b, start=True, stop=True),
                             reads=(bkfm[kvh], bqo[qh * 8 + c]), writes=(bpp[pq],))
                        P.op("dve", lambda eng, o=scbf[pq][:, :], p=pss[0:CH, 0:CH],
                             dg=diag[:, qh * CH:(qh + 1) * CH]:
                             eng.tensor_mul(o, p, dg),
                             reads=(bpp[pq], bconst), writes=(bscbf[pq],))
                        P.op("act", lambda eng, o=kvbf[pq][:, :],
                             i=kvsb[:, qh * DH:(qh + 1) * DH]: eng.copy(o, i),
                             reads=(bkv[qh],), writes=(bkvbf[pq],))
                        P.op("dve", lambda eng, o=qtil[pq][:, :],
                             q=hq[:, qh * 512 + c * CH: qh * 512 + c * CH + CH],
                             qd=qdec[:, qh * CH:(qh + 1) * CH]:
                             eng.tensor_mul(o, q, qd),
                             reads=(bqo[qh * 8 + c], bconst), writes=(bqtil[pq],))
                        pso = pp[2 + pq]
                        if ATTN_QH >= 2:
                            P.op("pe", lambda eng, a=vtok[pg][:, kvh * 128:(kvh + 1) * 128],
                                 b=scbf[pq][:, :], o=pso[:, 0:CH],
                                 t=(ATTN_QH < 3):
                                 eng.matmul(o, a, b, start=True, stop=t),
                                 reads=(bvtok[pg], bscbf[pq]), writes=(bpp[2 + pq],))
                        if ATTN_QH >= 3:
                            P.op("pe", lambda eng, a=kvbf[pq][:, :], b=qtil[pq][:, :],
                                 o=pso[:, 0:CH]:
                                 eng.matmul(o, a, b, start=False, stop=True),
                                 reads=(bkvbf[pq], bqtil[pq]), writes=(bpp[2 + pq],))
                        if ATTN_QH >= 2:
                            P.op("act", lambda eng, o=hq[:, qh * 512 + c * CH: qh * 512 + c * CH + CH],
                                 i=pso[:, 0:CH]: eng.copy(o, i),
                                 reads=(bpp[2 + pq],), writes=(bqo[qh * 8 + c],))
                    if ATTN_QH < 4 or phase == "skip":
                        continue
                    P.op("dve", lambda eng, o=ktil[pq][:, :],
                         k=ktok[pg][:, kvh * 128:(kvh + 1) * 128],
                         kd=kdec[:, qh:qh + 1]:
                         eng.tensor_scalar(o, k, kd, None, ALU.mult),
                         reads=(bktok[pg], bconst), writes=(bktil[pq],))
                    psu = pp[4 + pq]
                    P.op("pe", lambda eng, a=ktil[pq][:, :],
                         b=vtok[pg][:, kvh * 128:(kvh + 1) * 128], o=psu[:, 0:DH]:
                         eng.matmul(o, a, b, start=True, stop=True),
                         reads=(bktil[pq], bvtok[pg]), writes=(bpp[4 + pq],))
                    P.op("dve", lambda eng, kv=kvsb[:, qh * DH:(qh + 1) * DH],
                         p=psu[:, 0:DH], bl=blk[qh]:
                         eng.scalar_tensor_tensor(kv, kv, bl, p, ALU.mult, ALU.add),
                         reads=(bkv[qh], bpp[4 + pq]), writes=(bkv[qh],))
            if phase == "B" and ATTN_DBG >= 2 and "o" in ATTN_SUB:
                for db in range(DCH):
                    wi = next_wb()
                    src = wo_in[db].rearrange("c p f -> p c f")
                    P.op("pool", lambda eng,
                         d=wb[wi][:, 0:2048].rearrange("p (c f) -> p c f", f=128), s=src:
                         eng.dma_start(d, s), reads=(), writes=(bwb[wi],), inc=16)
                    psy = pp[4 + db % 2]
                    for eb in range(H):
                        P.op("pe", lambda eng, a=wb[wi][:, eb * 128:(eb + 1) * 128],
                             b=hq[:, eb * 512: eb * 512 + 512], o=psy[:, :],
                             s=(eb == 0), t=(eb == H - 1):
                             eng.matmul(o, a, b, start=s, stop=t),
                             reads=tuple([bwb[wi]] + [bqo[eb * 8 + cc] for cc in range(8)]),
                             writes=(bpp[4 + db % 2],))
                    xap = xsb[:, db * FT: db * FT + 512]
                    P.op("dve", lambda eng, x=xap, p=psy[:, :]:
                         eng.tensor_add(x, x, p),
                         reads=(bpp[4 + db % 2], bx[db]), writes=(bx[db],))
            if phase == "B":
                dst = x2 if not mode.startswith('attn') else yout
                dbufs = bx2 if not mode.startswith('attn') else bout
                d, s = store_x_tile(dst, tc, AT)
                P.op("sync", lambda eng, d=d, s=s: eng.dma_start(d, s),
                     reads=tuple(bx), writes=(dbufs[at],), inc=16)

        def emit_attn():
            P.op("dve", lambda eng: eng.memset(kvsb[:, :], 0.0),
                 writes=tuple(bkv))
            for at in range(4):
                emit_attn_phase("A", at)
            P.op("dve", lambda eng: eng.tensor_scalar(
                kvsb[:, :], kvsb[:, :], msend[:, 0:1], None, ALU.mult),
                reads=tuple(bkv) + (bconst,), writes=tuple(bkv))
            if mode != "attn_nocc":
                P.op("sync", lambda eng: eng.dma_start(cc_in, kvsb[:, :]),
                     reads=tuple(bkv), writes=(bccin,), inc=16)
                P.op("cc", lambda eng: eng.collective_compute(
                    "AllReduce", ALU.add,
                    replica_groups=[[0, 1], [2, 3], [4, 5], [6, 7]],
                    ins=[cc_in], outs=[cc_out]),
                    reads=(bccin,), writes=(bccout,), inc=1)
                P.op("sync", lambda eng: eng.dma_start(kvsb[:, :], cc_out),
                     reads=(bccout,) + tuple(bkv), writes=tuple(bkv), inc=16)
            P.op("dve", lambda eng: eng.tensor_scalar(
                kvsb[:, :], kvsb[:, :], mrecv[:, 0:1], None, ALU.mult),
                reads=tuple(bkv) + (bconst,), writes=tuple(bkv))
            for at in range(4):
                bqo = [rebind(P, bh) for _ in range(H * 8)]
                emit_attn_phase("B", at, bqo=bqo)
                newh = [rebind(P, bqo) for _ in range(FCH)]
                for i in range(FCH):
                    bh[i] = newh[i]

        if mode == "full":
            emit_ffn(xin, x1, [Buf() for _ in range(4)], bx1)
            emit_attn()
            emit_ffn(x2, yout, bx2, bout)
        elif mode == "ffn1":
            emit_ffn(xin, yout, [Buf() for _ in range(4)], bout)
        elif mode in ("attn", "attn_nocc"):
            emit_attn()
        P.op("sync", lambda eng: eng.nop(), reads=tuple(bout))

        P.replay(nc, sems)
    return nc



def _arr(a):
    return np.ascontiguousarray(a, dtype=np.float32)


def _prep_shared(inp):
    rw1 = inp["L0_ffn_routed_w1"]
    sw1 = inp["L0_ffn_shared_w1"]
    rw2 = inp["L0_ffn_routed_w2"]
    sw2 = inp["L0_ffn_shared_w2"]

    def w1_arr(w):
        n = w.shape[0]
        return _arr(np.transpose(w.reshape(n, DCH, 128, FCH, 128), (0, 3, 1, 2, 4)))

    def w2_arr(w):
        n = w.shape[0]
        return _arr(np.transpose(w.reshape(n, FCH, 128, DCH, 128), (0, 3, 1, 2, 4)))

    b1a = np.concatenate([np.asarray(inp["L0_ffn_routed_b1"]),
                          np.asarray(inp["L0_ffn_shared_b1"])], axis=0)
    b1a = _arr(np.transpose(b1a.reshape(NR + NS, FCH, 128), (2, 0, 1))
               .reshape(128, (NR + NS) * FCH))

    cent = np.asarray(inp["L0_ffn_centroids"])
    centT = _arr(cent.T.reshape(DCH, 128, NR))

    wq = np.asarray(inp["L0_attn_wq"])
    wk = np.asarray(inp["L0_attn_wk"])
    wv = np.asarray(inp["L0_attn_wv"])
    wo = np.asarray(inp["L0_attn_wo"])

    def wqT_arr(w, nh):
        wT = w.T
        return _arr(np.transpose(wT.reshape(DCH, 128, nh, 128), (2, 0, 1, 3)))

    woT = wo.T
    woT_arr = _arr(np.transpose(woT.reshape(H, 128, DCH, 128), (2, 0, 1, 3)))

    hs = np.arange(1, H + 1, dtype=np.float64)
    lam = np.exp2(-8.0 * hs / H)
    r = np.arange(CH, dtype=np.float64)
    diff = r[:, None] - r[None, :]
    diagT = np.zeros((CH, H * CH), dtype=np.float64)
    for h in range(H):
        m = np.where(diff >= 0, np.exp(-lam[h] * diff), 0.0)
        diagT[:, h * CH:(h + 1) * CH] = m.T
    qdec = np.zeros((128, H * CH), dtype=np.float64)
    for h in range(H):
        qdec[:, h * CH:(h + 1) * CH] = np.exp(-lam[h] * (r + 1.0))[None, :]
    kdec = np.zeros((CH, H), dtype=np.float64)
    for h in range(H):
        kdec[:, h] = np.exp(-lam[h] * (CH - 1.0 - r))

    eye = np.eye(128, dtype=np.float32)
    rot = np.zeros((128, 128), dtype=np.float32)
    for m in range(64):
        rot[m + 64, m] = -1.0
    for m in range(64, 128):
        rot[m - 64, m] = 1.0
    esel = np.zeros((NR, NR * 128), dtype=np.float32)
    for j in range(NR):
        esel[j, j * 128:(j + 1) * 128] = 1.0

    return dict(
        rw1=w1_arr(rw1), sw1=w1_arr(sw1), rw2=w2_arr(rw2), sw2=w2_arr(sw2),
        b1a=b1a, rb2=_arr(inp["L0_ffn_routed_b2"]), sb2=_arr(inp["L0_ffn_shared_b2"]),
        centT=centT, rbias=_arr(np.asarray(inp["L0_ffn_bias"]).reshape(NR, 1)),
        btile_in=_arr(np.broadcast_to(np.asarray(inp["L0_ffn_bias"])[None, :],
                                      (128, NR))),
        wqT=wqT_arr(wq, H), wkT=wqT_arr(wk, HKV), wvT=wqT_arr(wv, HKV),
        woT=woT_arr,
        diagT=_arr(diagT), qdec=_arr(qdec), kdec=_arr(kdec),
        eye=eye, rotR=rot, esel=esel,
        onesc=np.ones((128, 1), np.float32), onesr=np.ones((1, 128), np.float32),
    )


def _rope_tables(p0):
    d = DH
    inv = 1.0 / (10000.0 ** (np.arange(0, d, 2, dtype=np.float64) / d))
    t = np.arange(p0, p0 + NTOK, dtype=np.float64)
    fr = t[None, :] * np.concatenate([inv, inv])[:, None]
    return _arr(np.cos(fr)), _arr(np.sin(fr))


def kernel(L0_attn_x, L0_attn_wq, L0_attn_wk, L0_attn_wv, L0_attn_wo,
           L0_ffn_shared_w1, L0_ffn_shared_b1, L0_ffn_shared_w2, L0_ffn_shared_b2,
           L0_ffn_routed_w1, L0_ffn_routed_b1, L0_ffn_routed_w2, L0_ffn_routed_b2,
           L0_ffn_centroids, L0_ffn_bias, _mode="full"):
    inp = dict(L0_attn_x=L0_attn_x, L0_attn_wq=L0_attn_wq, L0_attn_wk=L0_attn_wk,
               L0_attn_wv=L0_attn_wv, L0_attn_wo=L0_attn_wo,
               L0_ffn_shared_w1=L0_ffn_shared_w1, L0_ffn_shared_b1=L0_ffn_shared_b1,
               L0_ffn_shared_w2=L0_ffn_shared_w2, L0_ffn_shared_b2=L0_ffn_shared_b2,
               L0_ffn_routed_w1=L0_ffn_routed_w1, L0_ffn_routed_b1=L0_ffn_routed_b1,
               L0_ffn_routed_w2=L0_ffn_routed_w2, L0_ffn_routed_b2=L0_ffn_routed_b2,
               L0_ffn_centroids=L0_ffn_centroids, L0_ffn_bias=L0_ffn_bias)
    shared = _prep_shared(inp)

    x = np.asarray(L0_attn_x, dtype=np.float32).reshape(4 * 4096, D)
    in_maps = []
    for c in range(NCORE):
        xs = _arr(x[c * NTOK:(c + 1) * NTOK, :].T)
        cos, sin = _rope_tables((c % 2) * NTOK)
        even = 1.0 if c % 2 == 0 else 0.0
        m = dict(shared)
        m["xT"] = xs
        m["costab"] = cos
        m["sintab"] = sin
        m["msend"] = np.full((128, 1), even, np.float32)
        m["mrecv"] = np.full((128, 1), 1.0 - even, np.float32)
        in_maps.append(m)

    nc = build_program(mode=_mode)
    res = run_bass_kernel_spmd(nc, in_maps, core_ids=list(range(NCORE)))
    out = np.empty((4 * 4096, D), dtype=np.float32)
    for c in range(NCORE):
        out[c * NTOK:(c + 1) * NTOK, :] = res.results[c]["y"].T
    return out.reshape(4, 4096, D)
